# version 11
# baseline (speedup 1.0000x reference)
import sys
sys.path.insert(0, '/opt/trn_rl_repo')
import contextlib
import numpy as np
import concourse.bass as bass
import concourse.mybir as mybir
from concourse.bass_utils import run_bass_kernel_spmd

F32 = mybir.dt.float32
BF16 = mybir.dt.bfloat16
Alu = mybir.AluOpType
Act = mybir.ActivationFunctionType
AX = mybir.AxisListType.X

C = 1024
HEADS = 16
HD = 64
FQKV = 3 * C
FFI = 2816
MAXLEN = 1024
NI = FFI // 128
NTHR = [0.25, 0.75, 1.25, 1.75, 2.5, 3.5, 5.0]
NW = [0.5, 0.5, 0.5, 0.5, 1.0, 1.0, 2.0]
GPSIMD_QUANT = True
OPT_AVLAG = True
OPT_DENOMLAG = True
OPT_PREPIL = True


class Prog:

    def __init__(self, nc):
        self.nc = nc
        self.ctx = contextlib.ExitStack()
        self.eng = {"v": nc.vector, "s": nc.scalar, "t": nc.tensor,
                    "g": nc.gpsimd, "q": nc.sync}
        self.semobj = {}
        self.cnt = {}
        self.is_dma = {}
        for e in "vstg":
            self.semobj[e] = self.ctx.enter_context(nc.semaphore("sem_" + e))
            self.cnt[e] = 0
            self.is_dma[e] = False
        self.know = {e: {} for e in "vstgq"}
        self.lastw = {}
        self.readers = {}
        self.snap = {}
        self.queue = {e: [] for e in "vstgq"}
        self.log = []

    def dma_sem(self, name):
        if name not in self.semobj:
            self.semobj[name] = self.ctx.enter_context(self.nc.semaphore(name))
            self.cnt[name] = 0
            self.is_dma[name] = True
        return name

    def _waits(self, e, reads, writes, fw=False):
        need = {}
        for b in reads:
            t = self.lastw.get(b)
            if t is not None:
                need[t[0]] = max(need.get(t[0], 0), t[1])
        for b in writes:
            t = self.lastw.get(b)
            if t is not None:
                need[t[0]] = max(need.get(t[0], 0), t[1])
            for rt in self.readers.get(b, ()):
                need[rt[0]] = max(need.get(rt[0], 0), rt[1])
        kn = self.know[e]
        for sname in sorted(need):
            val = need[sname]
            if sname == e and not fw and not getattr(self, 'FW_ALL', False):
                continue
            if self.is_dma[sname]:
                val = self.cnt[sname]
            if kn.get(sname, 0) >= val:
                continue
            mul = 16 if self.is_dma[sname] else 1
            self.queue[e].append(("w", sname, val * mul))
            sn = self.snap.get((sname, val))
            if sn:
                for k2, v2 in sn.items():
                    if kn.get(k2, 0) < v2:
                        kn[k2] = v2
            kn[sname] = max(kn.get(sname, 0), val)

    def _finish(self, e, semname, reads, writes, inc):
        self.cnt[semname] += 1
        tag = (semname, self.cnt[semname])
        sn = dict(self.know[e])
        sn[semname] = self.cnt[semname]
        if e in "vstg":
            sn[e] = self.cnt.get(e, 0)
        self.snap[tag] = sn
        for b in writes:
            self.lastw[b] = tag
            self.readers[b] = []
        for b in reads:
            self.readers.setdefault(b, []).append(tag)

    def op(self, e, f, reads=(), writes=(), fw=False):
        assert e in "vstg"
        n0 = len(self.queue[e])
        self._waits(e, reads, writes, fw=fw)
        self.log.append((e, "op", tuple(reads), tuple(writes),
                         tuple(self.queue[e][n0:])))
        sem = self.semobj[e]
        self.queue[e].append(("i", f, sem, 1))
        self._finish(e, e, reads, writes, 1)

    def dma(self, e, semname, out, in_, reads=(), writes=()):
        self.dma_sem(semname)
        n0 = len(self.queue[e])
        self._waits(e, reads, writes)
        self.log.append((e, "dma:" + semname, tuple(reads), tuple(writes),
                         tuple(self.queue[e][n0:])))
        sem = self.semobj[semname]
        self.queue[e].append(("d", out, in_, sem))
        self._finish(e, semname, reads, writes, 16)

    def barrier(self, e, semname):
        val = self.cnt.get(semname, 0)
        if val == 0:
            return
        kn = self.know[e]
        if kn.get(semname, 0) >= val:
            return
        mul = 16 if self.is_dma[semname] else 1
        self.queue[e].append(("w", semname, val * mul))
        kn[semname] = val

    def emit(self, e, engine):
        for item in self.queue[e]:
            if item[0] == "w":
                engine.wait_ge(self.semobj[item[1]], item[2])
            elif item[0] == "i":
                inst = item[1](engine)
                inst.then_inc(item[2], item[3])
            else:
                inst = engine.dma_start(out=item[1], in_=item[2])
                inst.then_inc(item[3], 16)


def build_program(T=2048, NB=2, dbg=False, ret_p=False):
    QCH = min(512, T)
    TC = QCH
    NT = T // 128
    NTC = T // TC
    NTQ = TC // 128
    NQC = T // QCH
    ZOFF = QCH - 1
    ZLEN = T + ZOFF
    DW = 127 + ZLEN
    NIC = (FFI + 511) // 512
    TW2 = T * 2 // 4
    assert T % 512 == 0 or T == QCH

    nc = bass.Bass("TRN2", target_bir_lowering=False, detect_race_conditions=False)
    p = Prog(nc)

    x_tm = nc.dram_tensor("x_tm", [NB, T, C], F32, kind="ExternalInput").ap()
    x_fm = nc.dram_tensor("x_fm", [NB, C, T], F32, kind="ExternalInput").ap()
    qkv_w = nc.dram_tensor("qkv_w", [FQKV, C], F32, kind="ExternalInput").ap()
    qkv_b = nc.dram_tensor("qkv_b", [1, FQKV], F32, kind="ExternalInput").ap()
    proj_w = nc.dram_tensor("proj_w", [C, C], F32, kind="ExternalInput").ap()
    proj_b = nc.dram_tensor("proj_b", [1, C], F32, kind="ExternalInput").ap()
    rel = nc.dram_tensor("rel", [2 * MAXLEN - 1, C], F32, kind="ExternalInput").ap()
    wg_d = nc.dram_tensor("wg", [FFI, C], F32, kind="ExternalInput").ap()
    wu_d = nc.dram_tensor("wu", [FFI, C], F32, kind="ExternalInput").ap()
    wd_d = nc.dram_tensor("wd", [C, FFI], F32, kind="ExternalInput").ap()
    out_fm = nc.dram_tensor("out_fm", [NB, C, T], F32, kind="ExternalOutput").ap()

    wqkvT_d = nc.dram_tensor("wqkvT_d", [8, 128, FQKV], BF16, kind="Internal").ap()
    projT_d = nc.dram_tensor("projT_d", [8, 128, C], BF16, kind="Internal").ap()
    wgqT_d = nc.dram_tensor("wgqT_d", [8, 128, FFI], BF16, kind="Internal").ap()
    wuqT_d = nc.dram_tensor("wuqT_d", [8, 128, FFI], BF16, kind="Internal").ap()
    wdqT_d = nc.dram_tensor("wdqT_d", [NI, 128, C], BF16, kind="Internal").ap()
    tcf_d = nc.dram_tensor("tcf_d", [HEADS, ZLEN], F32, kind="Internal")
    dbias_d = nc.dram_tensor("dbias_d", [HEADS, 128, DW], BF16, kind="Internal")
    xnqT_d = nc.dram_tensor("xnqT_d", [NB, 8, 128, T], BF16, kind="Internal").ap()
    ppart_d = nc.dram_tensor("ppart_d", [NB, 8, 128, T], F32, kind="Internal").ap()

    MTOTAL = [0]

    def mk(name, shape, dtype, off, seg=0):
        nbytes = shape[1] * (2 if dtype == BF16 else 4)
        start = MBASE + seg + off
        MTOTAL[0] = max(MTOTAL[0], seg + off + nbytes)
        return nc.alloc_sbuf_tensor_at(name, list(shape), dtype, offset=start)

    qkT_b0 = 4096 if T >= 2048 else T * 2
    _A_sz = max(16 * qkT_b0 + 16 * 2112, 16 * 5632, 10240 + 8192)
    _SW = ZLEN - 127
    _Xoff = 8 * qkT_b0 + 2 * _SW * 2
    _B_sz = max(_Xoff + 8192, 22 * 1024 + 4096 + 11264 + 3 * 4096)
    _TOT = _A_sz + _B_sz + 4096 + 28672 + 24576
    MBASE = (nc.sbuf_top - _TOT) // 32 * 32

    SA = 0
    qkT_b = qkT_b0
    A_attn = 16 * qkT_b + 16 * 2112
    A_ffn = 16 * 5632
    A_sz = max(A_attn, A_ffn, 10240 + 8192)
    qkT = [mk(f"qkT{i}", [128, T], BF16, i * qkT_b, SA) for i in range(16)]
    vaug = [mk(f"vaug{i}", [128, HEADS * 65], BF16, 16 * qkT_b + i * 2112, SA)
            for i in range(NT)]
    wgq = [mk(f"wgq{i}", [128, FFI], BF16, i * 5632, SA) for i in range(8)]
    wuq = [mk(f"wuq{i}", [128, FFI], BF16, 45056 if False else 8 * 5632 + i * 5632, SA)
           for i in range(8)] if A_sz >= 16 * 5632 else None
    if wuq is None:
        wuq = [mk(f"wuq{i}", [128, FFI], BF16, 8 * 5632 + i * 5632, SA) for i in range(8)]
    tcf = mk("tcf", [128, ZLEN], F32, 0, SA)
    tblT = mk("tblT", [128, 2048], F32, ZLEN * 4 + 32 - (ZLEN * 4) % 32, SA)
    A_P0 = ["tblT", "tcf"]
    A_ATTN = [f"qkT{i}" for i in range(16)] + [f"vaug{i}" for i in range(NT)]
    A_FFN = [f"wgq{i}" for i in range(8)] + [f"wuq{i}" for i in range(8)]

    SB = SA + A_sz
    SW = ZLEN - 127
    xaT_b = 4096 if T >= 2048 else T * 2
    ds_b = SW * 2
    Xoff = 8 * xaT_b + 2 * ds_b
    B_attn = Xoff + 8192
    B_ffn = 22 * 1024 + 4096 + 11264 + 3 * 4096
    B_sz = max(B_attn, B_ffn)
    xaT = [mk(f"xaT{i}", [128, T], BF16, i * xaT_b, SB) for i in range(8)]
    dstrip = [mk(f"dstrip{i}", [128, SW], BF16, 8 * xaT_b + i * ds_b, SB) for i in range(2)]
    wqbuf = [mk(f"wqbuf{i}", [128, 1024], BF16, Xoff + i * 2048, SB) for i in range(2)]
    vwbuf = [mk(f"vwbuf{i}", [128, 512], BF16, Xoff + i * 1024, SB) for i in range(8)]
    attT = [mk(f"attT{i}", [128, QCH], BF16, Xoff + i * 1024, SB) for i in range(3)]
    exps = [mk(f"exps{i}", [128, QCH], BF16, Xoff + 4928 + i * 1024, SB) for i in range(2)]
    avn = mk("avn", [128, QCH], BF16, Xoff + 3 * 1024, SB)
    avn2 = mk("avn2", [128, QCH], BF16, Xoff + 6976, SB)
    projbuf = [mk(f"projbuf{i}", [128, 1024], BF16, Xoff + i * 2048, SB) for i in range(2)]
    ident_f = mk("ident_f", [128, 128], F32, Xoff + 4096, SB)
    onesb = mk("onesb", [128, 128], BF16, Xoff + 4608, SB)
    tstage = mk("tstage", [128, 16], F32, Xoff + 4864, SB)
    hqT = [mk(f"hqT{i}", [128, 512], BF16, i * 1024, SB) for i in range(NI)]
    xnq_s = [mk(f"xnq_s{i}", [128, 128], BF16, 22 * 1024 + i * 256, SB) for i in range(16)]
    wdbuf = [mk(f"wdbuf{i}", [128, FFI], BF16, 26624 + i * 5632, SB) for i in range(2)]
    ppbuf = [mk(f"ppbuf{i}", [128, 512], F32, 37888 + i * 2048, SB) for i in range(2)]
    xfbuf = [mk(f"xfbuf{i}", [128, 512], F32, 41984 + i * 2048, SB) for i in range(2)]
    ostage = [mk(f"ostage{i}", [128, 512], F32, 46080 + i * 2048, SB) for i in range(2)]
    qdqw = mk("qdqw", [128, FFI], BF16, 26624, SB)
    X_Q = ["wqbuf0", "wqbuf1"] + [f"vwbuf{i}" for i in range(8)]
    X_A = ["attT0", "attT1", "attT2", "avn", "avn2", "attf0", "attf1",
           "exps0", "exps1"]
    X_P = ["pstage0", "pstage1", "projbuf0", "projbuf1"]
    B_ATTN = ([f"xaT{i}" for i in range(8)] + ["dstrip0", "dstrip1"] + X_Q + X_A + X_P
              + ["qdqw", "ident_f", "onesb", "tstage"])
    B_FFN = ([f"hqT{i}" for i in range(NI)] + ["ppbuf0", "ppbuf1", "xfbuf0", "xfbuf1",
              "ostage0", "ostage1", "wdbuf0", "wdbuf1"] + [f"xnq_s{i}" for i in range(16)])

    SC = SB + B_sz
    attf = [mk(f"attf{i}", [128, QCH], F32, i * 2048, SC) for i in range(2)]
    pstage = [mk(f"pstage{i}", [128, 512], F32, i * 2048, SC) for i in range(2)]
    B32_sz = 4096

    SD = SC + B32_sz
    vt_ = mk("vt", [128, FFI], F32, 0, SD)
    fbuf0 = mk("fbuf0", [128, FFI], F32, 11264, SD)
    fcast = mk("fcast", [128, FFI], BF16, 11264, SD)
    fcast2 = mk("fcast2", [128, FFI], BF16, 16896, SD)
    xtile = [mk(f"xtile{i}", [128, C], F32, 11264 + i * 4096, SD) for i in range(2)]
    gs_b = mk("gs_b", [128, 512], F32, 22528, SD)
    xn = mk("xn", [128, C], F32, 24576, SD)
    C32_sz = 28672
    dnr = gs_b
    dnb = xn

    SE = SD + C32_sz
    qnb = mk("qnb", [128, FFI], BF16, 0, SE)
    cmb = mk("cmb", [128, FFI], BF16, 5632, SE)
    sgn_b = mk("sgn", [128, FFI], BF16, 11264, SE)
    wstage2 = mk("wstage2", [128, FFI], BF16, 16896, SE)
    wstage = mk("wstage", [128, 1024], BF16, 16896, SE)
    wstage3 = mk("wstage3", [128, 1024], BF16, 18944, SE)
    tcb16 = mk("tcb16", [128, ZLEN], BF16, 16896, SE)
    xnb = mk("xnb", [128, C], BF16, 22528, SE)
    qdq = mk("qdq", [128, C], BF16, 22528, SE)
    C16_sz = 24576

    mblk = p.ctx.enter_context(nc.sbuf_tensor("mblk", [128, FFI // 16], F32))
    sblk = p.ctx.enter_context(nc.sbuf_tensor("sblk", [128, FFI // 16], F32))
    iblk = p.ctx.enter_context(nc.sbuf_tensor("iblk", [128, FFI // 16], F32))
    bns = p.ctx.enter_context(nc.sbuf_tensor("bns", [128, 12], F32))
    bna = p.ctx.enter_context(nc.sbuf_tensor("bna", [128, 2], F32))
    aff = p.ctx.enter_context(nc.sbuf_tensor("aff", [128, 4], F32))
    qkvb_c = p.ctx.enter_context(nc.sbuf_tensor("qkvb_c", [128, 24], F32))
    projb_c = p.ctx.enter_context(nc.sbuf_tensor("projb_c", [128, 8], F32))
    vbb = p.ctx.enter_context(nc.sbuf_tensor("vbb", [128, C], BF16))
    ident_b = p.ctx.enter_context(nc.sbuf_tensor("ident_b", [128, 128], BF16))
    ones64 = p.ctx.enter_context(nc.sbuf_tensor("ones64", [128, 64], F32))

    PS = [p.ctx.enter_context(nc.psum_tensor(f"ps{i}", [128, 512], F32)) for i in range(6)]
    PS6B = p.ctx.enter_context(nc.psum_tensor("ps6b", [128, 1024], BF16))
    PS7B = p.ctx.enter_context(nc.psum_tensor("ps7b", [128, 1024], BF16))
    PS = PS + [PS6B, PS7B]
    PSB = [None] * 6 + [PS6B[:, 0:128], PS7B[:, 0:128]]

    blk = p.ctx.enter_context(nc.Block())

    p.op("g", lambda g: g.memset(onesb[:], 1.0), writes=["onesb"])
    p.op("g", lambda g: g.affine_select(ident_b[:], onesb[:], pattern=[[-1, 128]],
         compare_op=Alu.is_equal, fill=0.0, base=0, channel_multiplier=1),
         reads=["onesb"], writes=["ident_b"])
    p.op("g", lambda g: g.affine_select(ident_f[:], onesb[:], pattern=[[-1, 128]],
         compare_op=Alu.is_equal, fill=0.0, base=0, channel_multiplier=1),
         reads=["onesb"], writes=["ident_f"])
    p.op("v", lambda v: v.memset(ones64[:], 1.0), writes=["ones64"])

    qb_src = bass.AP(tensor=qkv_b.tensor, offset=0, ap=[[128, 24], [1, 128]])
    p.dma("q", "ld0", gs_b[0:24, 0:128], qb_src, writes=["gs_b"])
    p.op("t", lambda t: t.transpose(PS[5][:, 0:24], gs_b[0:24, 0:128], ident_f[0:24, 0:24]),
         reads=["gs_b", "ident_f"], writes=["PS5"])
    p.op("s", lambda s: s.copy(qkvb_c[:], PS[5][:, 0:24]), reads=["PS5"], writes=["qkvb_c"])
    pb_src = bass.AP(tensor=proj_b.tensor, offset=0, ap=[[128, 8], [1, 128]])
    p.dma("q", "ld0", gs_b[0:8, 128:256], pb_src, writes=["gs_b"])
    p.op("t", lambda t: t.transpose(PS[5][:, 0:8], gs_b[0:8, 128:256], ident_f[0:8, 0:8]),
         reads=["gs_b", "ident_f"], writes=["PS5"])
    p.op("s", lambda s: s.copy(projb_c[:], PS[5][:, 0:8]), reads=["PS5"], writes=["projb_c"])
    p.dma("g", "ldvb", vbb[:], qkv_b[0, 2 * C:3 * C].unsqueeze(0).broadcast_to([128, C]),
          writes=["vbb"])

    for wt in range(24 + 8):
        is_proj = wt >= 24
        src = proj_w if is_proj else qkv_w
        row = (wt - 24 if is_proj else wt) * 128
        fcb, fck = (fcast, "fcast") if wt % 2 == 0 else (fcast2, "fcast2")
        wsb, wsk = (wstage, "wstage") if wt % 2 == 0 else (wstage3, "wstage3")
        p.dma("g", "ldwg", fcb[:, 0:C], src[row:row + 128, :], writes=[fck])
        for ct in range(8):
            pb = ct % 6
            p.op("t", lambda t, ct=ct, pb=pb, fcb=fcb: t.matmul(
                PS[pb][:, 0:128], fcb[:, ct * 128:(ct + 1) * 128], ident_b[:],
                start=True, stop=True),
                reads=[fck, "ident_b"], writes=[f"PS{pb}"])
            p.op("s", lambda s, ct=ct, pb=pb, wsb=wsb: s.copy(
                wsb[:, ct * 128:(ct + 1) * 128], PS[pb][:, 0:128]),
                reads=[f"PS{pb}"], writes=[wsk])
        dstd = projT_d if is_proj else wqkvT_d
        dstk = "projT_dram" if is_proj else "wqkvT_dram"
        p.dma("q", "stw", dstd[:, :, row:row + 128].rearrange("c p n -> p c n"),
              wsb[:].rearrange("p (c n) -> p c n", c=8),
              reads=[wsk], writes=[dstk])

    nrt = (2 * MAXLEN - 1 + 127) // 128
    for rt in range(nrt):
        row0 = min(rt * 128, 2 * MAXLEN - 1 - 128)
        p.dma("q", "ldr", fbuf0[:, 0:C], rel[row0:row0 + 128, :],
              writes=["fbuf0"] + (["fcast", "fcast2"] if row0 == 0 else []))
        p.op("v", lambda v: v.tensor_reduce(
            tstage[:, 0:16], fbuf0[:, 0:C].rearrange("p (h d) -> p h d", d=HD),
            axis=AX, op=Alu.add), reads=["fbuf0"], writes=["tstage"])
        p.op("t", lambda t: t.transpose(PS[5][0:16, 0:128], tstage[:, 0:16], ident_f[:]),
             reads=["tstage", "ident_f"], writes=["PS5"])
        p.op("s", lambda s, row0=row0: s.copy(tblT[0:16, row0:row0 + 128],
                                              PS[5][0:16, 0:128]),
             reads=["PS5"], writes=["tblT"])
    p.op("v", lambda v: v.memset(tcf[0:16, 0:ZOFF], -30000.0), writes=["tcf"])
    biaslen = min(1022, T - 1) + 1
    p.op("v", lambda v: v.tensor_copy(tcf[0:16, ZOFF:ZOFF + biaslen],
                                      tblT[0:16, 1023:1023 + biaslen]),
         reads=["tblT"], writes=["tcf"])
    if ZOFF + biaslen < ZLEN:
        fl = ZLEN - ZOFF - biaslen
        p.op("v", lambda v, fl=fl: v.tensor_scalar(
            tcf[0:16, ZOFF + biaslen:ZLEN], tcf[0:16, 0:fl], 0.0,
            tblT[0:16, 2046:2047], op0=Alu.mult, op1=Alu.add),
            reads=["tblT", "tcf"], writes=["tcf"])
    p.op("s", lambda s: s.activation(tcf[0:16, 0:ZLEN], tcf[0:16, 0:ZLEN], Act.Exp),
         reads=["tcf"], writes=["tcf"], fw=True)
    p.dma("q", "sttc", tcf_d.ap()[:, :], tcf[0:16, 0:ZLEN], reads=["tcf"],
          writes=["tcf_dram"])
    for h in range(HEADS):
        p.dma("g", "ldtc", tcb16[:],
              tcf_d.ap()[h, :].unsqueeze(0).broadcast_to([128, ZLEN]),
              reads=["tcf_dram"],
              writes=["wstage2"] + (["wstage", "wstage3"] if h == 0 else []))
        ddst = bass.AP(tensor=dbias_d, offset=h * 128 * DW, ap=[[DW + 1, 128], [1, ZLEN]])
        p.dma("q", "std", ddst, tcb16[:], reads=["wstage2"], writes=["dbias_dram"])

    def quant(src_key, src_ap, width, dst_ap, dst_key, extra_reads=()):
        er = list(extra_reads)
        nch = 2 if width > 1024 else 1
        cw = width // nch
        first = True
        for c in range(nch):
            nb = cw // 16
            b0 = c * (cw // 16)
            x3 = src_ap[:, c * cw:(c + 1) * cw].rearrange("p (b k) -> p b k", k=16)
            sl = (slice(None), slice(c * cw, (c + 1) * cw))
            mb = mblk[:, b0:b0 + nb]
            sb_ = sblk[:, b0:b0 + nb]
            ibl = iblk[:, b0:b0 + nb]
            mk_, sk_, ik_ = f"mblk{c}", f"sblk{c}", f"iblk{c}"
            p.op("v", lambda v, x3=x3, mb=mb, nb=nb: v.tensor_reduce(
                mb, x3, axis=AX, op=Alu.max, apply_absolute_value=True),
                reads=[src_key] + (er if first else []), writes=[mk_])
            p.op("v", lambda v, mb=mb, sb_=sb_: v.tensor_scalar(
                sb_, mb, 1.0 / 6.0, 1e-12, op0=Alu.mult, op1=Alu.max),
                reads=[mk_], writes=[sk_])
            p.op("v", lambda v, sb_=sb_, ibl=ibl: v.reciprocal(ibl, sb_),
                 reads=[sk_], writes=[ik_])
            ib3 = ibl.unsqueeze(2).broadcast_to([128, nb, 16])
            sb3 = sb_.unsqueeze(2).broadcast_to([128, nb, 16])
            vtc = vt_[sl]
            p.op("g", lambda g, x3=x3, ib3=ib3, vtc=vtc: g.tensor_tensor(
                vtc.rearrange("p (b k) -> p b k", k=16), x3, ib3, op=Alu.mult),
                reads=[src_key, ik_] + (er if first else []), writes=[f"vt{c}"])
            p.op("v", lambda v, vtc=vtc, c=c, cw=cw: v.tensor_scalar(
                cmb[:, c * cw:(c + 1) * cw], vtc, 0.0, 2.0,
                op0=Alu.is_gt, op1=Alu.mult), reads=[f"vt{c}"], writes=[f"cmb{c}"])
            p.op("v", lambda v, c=c, cw=cw: v.tensor_scalar(
                sgn_b[:, c * cw:(c + 1) * cw], cmb[:, c * cw:(c + 1) * cw], -1.0,
                None, op0=Alu.add), reads=[f"cmb{c}"], writes=[f"sgn{c}"])
            p.op("s", lambda s, vtc=vtc: s.square(vtc, vtc),
                 reads=[f"vt{c}"], writes=[f"vt{c}"], fw=False)
            p.op("v", lambda v, vtc=vtc, c=c, cw=cw: v.tensor_scalar(
                qnb[:, c * cw:(c + 1) * cw], vtc, NTHR[0] ** 2, NW[0],
                op0=Alu.is_gt, op1=Alu.mult), reads=[f"vt{c}"], writes=[f"qnb{c}"])
            for i in range(1, 7):
                p.op("v", lambda v, i=i, vtc=vtc, c=c, cw=cw: v.tensor_scalar(
                    cmb[:, c * cw:(c + 1) * cw], vtc, NTHR[i] ** 2, NW[i],
                    op0=Alu.is_gt, op1=Alu.mult), reads=[f"vt{c}"], writes=[f"cmb{c}"])
                p.op("v", lambda v, c=c, cw=cw: v.tensor_tensor(
                    qnb[:, c * cw:(c + 1) * cw], qnb[:, c * cw:(c + 1) * cw],
                    cmb[:, c * cw:(c + 1) * cw], op=Alu.add),
                    reads=[f"cmb{c}"], writes=[f"qnb{c}"])
            p.op("v", lambda v, c=c, cw=cw: v.tensor_tensor(
                qnb[:, c * cw:(c + 1) * cw], qnb[:, c * cw:(c + 1) * cw],
                sgn_b[:, c * cw:(c + 1) * cw], op=Alu.mult),
                reads=[f"sgn{c}"], writes=[f"qnb{c}"])
            p.op("g", lambda g, c=c, cw=cw, sb3=sb3, nb=nb: g.tensor_tensor(
                dst_ap[:, c * cw:(c + 1) * cw].rearrange("p (b k) -> p b k", k=16),
                qnb[:, c * cw:(c + 1) * cw].rearrange("p (b k) -> p b k", k=16),
                sb3, op=Alu.mult),
                reads=[f"qnb{c}", sk_],
                writes=[dst_key] + (["xnb"] if dst_key == "qdq" and first else []))
            first = False

    def prep_tile(src_dram, wt, width, nct, dst_dram, dkey, p0=False):
        st = wstage if nct == 8 else wstage2
        stk = "wstage" if nct == 8 else "wstage2"
        dst_q = qdqw if width == FFI else qdq
        dqk = "qdqw" if width == FFI else "qdq"
        p.dma("q", "ldf", fbuf0[:, 0:width], src_dram[wt * 128:(wt + 1) * 128, :],
              writes=["fbuf0"])
        quant("fbuf0", fbuf0[:, 0:width], width, dst_q[:, 0:width], dqk)
        for ct in range(nct):
            if p0:
                pb = ct % 6
                p.op("t", lambda t, ct=ct, pb=pb, dst_q=dst_q: t.matmul(
                    PS[pb][:, 0:128], dst_q[:, ct * 128:(ct + 1) * 128], ident_b[:],
                    start=True, stop=True),
                    reads=[dqk, "ident_b"], writes=[f"PS{pb}"])
                p.op("s", lambda s, ct=ct, pb=pb, st=st: s.copy(
                    st[:, ct * 128:(ct + 1) * 128], PS[pb][:, 0:128]),
                    reads=[f"PS{pb}"], writes=[stk])
                continue
            pb = 6 + (ct % 2)
            p.op("t", lambda t, ct=ct, pb=pb, dst_q=dst_q: t.transpose(
                PSB[pb], dst_q[:, ct * 128:(ct + 1) * 128], ident_b[:]),
                reads=[dqk, "ident_b"], writes=[f"PS{pb}"])
            p.op("s", lambda s, ct=ct, pb=pb, st=st: s.copy(
                st[:, ct * 128:(ct + 1) * 128], PSB[pb]),
                reads=[f"PS{pb}"], writes=[stk])
        p.dma("q", "stw", dst_dram[:, :, wt * 128:(wt + 1) * 128].rearrange("c p n -> p c n"),
              st[:, 0:nct * 128].rearrange("p (c n) -> p c n", c=nct),
              reads=[stk], writes=[dkey])

    for wt in range(8):
        prep_tile(wd_d, wt, FFI, NI, wdqT_d, "wdq_dram", p0=True)
    ffnw_tiles = ([(wg_d, wt, C, 8, wgqT_d, "wgq_dram") for wt in range(NI)]
                  + [(wu_d, wt, C, 8, wuqT_d, "wuq_dram") for wt in range(NI)])
    ffnw_iter = iter(ffnw_tiles)

    for b in range(NB):
        for tt in range(NT):
            fb = tt % 2
            if tt == 0:
                extraw = (A_FFN + B_FFN) if b > 0 else ["qdqw"]
            else:
                extraw = []
            p.dma("q", f"ldx{fb}", xtile[fb][:], x_tm[b, tt * 128:(tt + 1) * 128, :],
                  writes=[f"xtile{fb}"] + (["fbuf0"] if tt < 2 else []))
            p.op("v", lambda v, fb=fb: v.bn_stats(bns[:, 0:6], xtile[fb][:, 0:512]),
                 reads=[f"xtile{fb}"], writes=["bns"])
            p.op("v", lambda v, fb=fb: v.bn_stats(bns[:, 6:12], xtile[fb][:, 512:1024]),
                 reads=[f"xtile{fb}"], writes=["bns"])
            p.op("v", lambda v: v.bn_aggr(bna[:], bns[:]), reads=["bns"], writes=["bna"],
                 fw=True)
            p.op("v", lambda v: v.tensor_scalar(aff[:, 2:3], bna[:, 1:2], 1e-5, None,
                 op0=Alu.add), reads=["bna"], writes=["aff2"], fw=True)
            p.op("s", lambda s: s.sqrt(aff[:, 3:4], aff[:, 2:3]), reads=["aff2"],
                 writes=["aff3"])
            p.op("v", lambda v: v.reciprocal(aff[:, 0:1], aff[:, 3:4]),
                 reads=["aff3"], writes=["aff0"], fw=True)
            p.op("v", lambda v: v.tensor_tensor(aff[:, 1:2], bna[:, 0:1], aff[:, 0:1],
                 op=Alu.mult), reads=["bna", "aff0"], writes=["aff1"], fw=True)
            p.op("v", lambda v: v.tensor_scalar(aff[:, 1:2], aff[:, 1:2], -1.0, None,
                 op0=Alu.mult), reads=["aff1"], writes=["aff1"], fw=True)
            p.op("v", lambda v, fb=fb: v.tensor_scalar(xn[:, 0:C], xtile[fb][:],
                 aff[:, 0:1], aff[:, 1:2], op0=Alu.mult, op1=Alu.add),
                 reads=[f"xtile{fb}", "aff0", "aff1"],
                 writes=["xn"] + (["dnb0", "dnb1", "dnr1"] if tt == 0 else []),
                 fw=True)
            p.op("v", lambda v: v.tensor_copy(xnb[:, 0:C], xn[:, 0:C]),
                 reads=["xn"], writes=["xnb"])
            for ct in range(8):
                pb = ct % 6
                p.op("t", lambda t, ct=ct, pb=pb: t.matmul(
                    PS[pb][:, 0:128], xnb[:, ct * 128:(ct + 1) * 128], ident_b[:],
                    start=True, stop=True),
                    reads=["xnb", "ident_b"],
                    writes=[f"PS{pb}"] + (["PS5a", "PS5b"] if pb == 5 else []))
                p.op("s", lambda s, ct=ct, tt=tt, pb=pb: s.copy(
                    xaT[ct][:, tt * 128:(tt + 1) * 128], PS[pb][:, 0:128]),
                    reads=[f"PS{pb}"], writes=[f"xaT{ct}"] + (extraw if ct == 0 else []))
            quant("xn", xn[:, 0:C], C, qdq[:, 0:C], "qdq")
            for ct in range(8):
                pb = (ct + 2) % 6
                p.op("t", lambda t, ct=ct, pb=pb: t.matmul(
                    PS[pb][:, 0:128], qdq[:, ct * 128:(ct + 1) * 128], ident_b[:],
                    start=True, stop=True),
                    reads=["qdq", "ident_b"],
                    writes=[f"PS{pb}"] + (["PS5a", "PS5b"] if pb == 5 else []))
                p.op("s", lambda s, ct=ct, pb=pb: s.copy(
                    wstage[:, ct * 128:(ct + 1) * 128], PS[pb][:, 0:128]),
                    reads=[f"PS{pb}"], writes=["wstage"])
            p.dma("q", "stxq", xnqT_d[b, :, :, tt * 128:(tt + 1) * 128].rearrange("c p n -> p c n"),
                  wstage[:].rearrange("p (c n) -> p c n", c=8),
                  reads=["wstage"], writes=["xnq_dram"])

        for ft in range(16):
            fb = ft % 2
            extraw = (X_A + X_P + [f"vwbuf{i}" for i in range(8)]) if (ft == 0 and b > 0) else []
            p.dma("q", f"ldq{fb}",
                  wqbuf[fb][:].rearrange("p (c n) -> p c n", c=8),
                  wqkvT_d[:, :, ft * 128:(ft + 1) * 128].rearrange("c p n -> p c n"),
                  reads=["wqkvT_dram"],
                  writes=[f"wqbuf{fb}"] + extraw)
            for tc4 in range(NTC):
                pb = (ft + tc4) % 2
                for ct in range(8):
                    p.op("t", lambda t, ft=ft, tc4=tc4, ct=ct, fb=fb, pb=pb: t.matmul(
                        PS[pb][:, 0:TC], wqbuf[fb][:, ct * 128:(ct + 1) * 128],
                        xaT[ct][:, tc4 * TC:(tc4 + 1) * TC],
                        start=(ct == 0), stop=(ct == 7)),
                        reads=[f"wqbuf{fb}", f"xaT{ct}"], writes=[f"PS{pb}"])
                is_q = ft < 8
                exw = [f"qkT{ft}"]
                if tc4 == 0 and ft == 0:
                    exw = exw + (A_FFN if b > 0 else A_P0)
                p.op("v", lambda v, ft=ft, tc4=tc4, pb=pb, is_q=is_q: v.tensor_scalar(
                    qkT[ft][:, tc4 * TC:(tc4 + 1) * TC], PS[pb][:, 0:TC],
                    qkvb_c[:, ft:ft + 1], (0.125 if is_q else 1.0),
                    op0=Alu.add, op1=Alu.mult),
                    reads=[f"PS{pb}", "qkvb_c"], writes=exw)
        for tt in range(NT):
            p.op("v", lambda v, tt=tt: v.memset(vaug[tt][:], 1.0), writes=[f"vaug{tt}"])
        for fc in range(2):
            for ct in range(8):
                exv = (["wqbuf0", "wqbuf1", "ident_f", "onesb", "tstage"]
                       if (fc == 0 and ct == 0) else [])
                p.dma("q", "ldvw", vwbuf[ct][:],
                      wqkvT_d[ct, :, 2 * C + fc * 512:2 * C + (fc + 1) * 512],
                      reads=["wqkvT_dram"], writes=[f"vwbuf{ct}"] + exv)
            for tt in range(NT):
                pb = tt % 2
                for ct in range(8):
                    p.op("t", lambda t, tt=tt, ct=ct, pb=pb: t.matmul(
                        PS[pb][:, 0:512], xaT[ct][:, tt * 128:(tt + 1) * 128],
                        vwbuf[ct][:], start=(ct == 0), stop=(ct == 7)),
                        reads=[f"vwbuf{ct}", f"xaT{ct}"], writes=[f"PS{pb}"])
                p.op("v", lambda v, tt=tt, fc=fc, pb=pb: v.tensor_tensor(
                    vaug[tt][:].rearrange("p (h e) -> p h e", e=65)[:, fc * 8:(fc + 1) * 8, 0:64],
                    PS[pb][:, 0:512].rearrange("p (h d) -> p h d", d=64),
                    vbb[:, fc * 512:(fc + 1) * 512].rearrange("p (h d) -> p h d", d=64),
                    op=Alu.add), reads=[f"PS{pb}", "vbb"], writes=[f"vaug{tt}"])

        if dbg and b == 0:
            dq1 = nc.dram_tensor("dbg_k0", [128, 512], BF16, kind="ExternalOutput").ap()
            p.dma("q", "stdbg", dq1, qkT[8][:, 0:512], reads=["qkT8"])
            dq2 = nc.dram_tensor("dbg_q0", [128, 512], BF16, kind="ExternalOutput").ap()
            p.dma("q", "stdbg", dq2, qkT[0][:, 0:512], reads=["qkT0"])
            dq3 = nc.dram_tensor("dbg_va0", [128, HEADS * 65], BF16, kind="ExternalOutput").ap()
            p.dma("q", "stdbg", dq3, vaug[0][:], reads=["vaug0"])

        def emit_score(h, qc, bs):
            db = h % 2
            ktile, qtile, poff = 8 + h // 2, h // 2, (h % 2) * 64
            sb_ = bs % 3
            p.op("t", lambda t, bs=bs, qc=qc, sb_=sb_, ktile=ktile, qtile=qtile,
                 poff=poff: t.matmul(
                     PS[sb_][:, 0:QCH],
                     qkT[ktile][poff:poff + 64, bs * 128:(bs + 1) * 128],
                     qkT[qtile][poff:poff + 64, qc * QCH:(qc + 1) * QCH],
                     start=True, stop=True),
                 reads=[f"qkT{ktile}", f"qkT{qtile}"], writes=[f"PS{sb_}"])
            c0 = qc * QCH - 128 * bs + ZOFF - 127
            ab = bs % 2
            exq = X_Q if (h == 0 and qc == 0 and bs == 0) else []
            p.op("s", lambda s, sb_=sb_, ab=ab: s.activation(
                exps[ab][:, 0:QCH], PS[sb_][:, 0:QCH], Act.Exp),
                reads=[f"PS{sb_}"], writes=[f"exps{ab}"] + exq)
            at = bs % 3
            p.op("v", lambda v, ab=ab, at=at, db=db, c0=c0: v.tensor_tensor(
                attT[at][:, 0:QCH], exps[ab][:, 0:QCH],
                dstrip[db][:, c0:c0 + QCH], op=Alu.mult),
                reads=[f"exps{ab}", f"dstrip{db}"], writes=[f"attT{at}"])

        def emit_av(h, qc, bs, nbs):
            avb = 3 + (qc % 2)
            at = bs % 3
            p.op("t", lambda t, bs=bs, at=at, avb=avb, h=h, nbs=nbs: t.matmul(
                PS[avb][0:65, 0:QCH], vaug[bs][:, h * 65:h * 65 + 65],
                attT[at][:, 0:QCH], start=(bs == 0), stop=(bs == nbs - 1)),
                reads=[f"vaug{bs}", f"attT{at}"], writes=[f"PS{avb}"])

        def emit_denom(h, qc):
            avb = 3 + (qc % 2)
            poff = (h % 2) * 64
            dp = (h * NQC + qc) % 2
            dnr_ap = (gs_b if dp == 0 else xn)[64:65, 0:QCH]
            dnrk = f"dnr{dp}"
            dnb_ap = xn[0:64, 0:QCH] if dp == 0 else xn[0:64, QCH:2 * QCH]
            dnbk = f"dnb{dp}"
            ps5h = PS[5][0:64, 0:QCH] if dp == 0 else PS[5][64:128, 0:QCH]
            ps5k = f"PS5{'ab'[dp]}"
            ones_ap = ones64[64:65, :]
            avn_ap = (avn if dp == 0 else avn2)[0:64, 0:QCH]
            avnk = "avn" if dp == 0 else "avn2"
            first = dn_first.pop(dp, False)
            exd = (["gs_b", "xn", "PS5"] if first else [])
            exa = ((["vwbuf6", "vwbuf7"] if dp else []) + X_Q if first else [])
            p.op("s", lambda s, avb=avb, dnr_ap=dnr_ap: s.copy(
                dnr_ap, PS[avb][64:65, 0:QCH]),
                reads=[f"PS{avb}"], writes=[dnrk] + exd)
            p.op("v", lambda v, dnr_ap=dnr_ap: v.reciprocal(dnr_ap, dnr_ap),
                 reads=[dnrk], writes=[dnrk])
            p.op("t", lambda t, ps5h=ps5h, dnr_ap=dnr_ap, ones_ap=ones_ap: t.matmul(
                ps5h, ones_ap, dnr_ap, start=True, stop=True),
                 reads=[dnrk, "ones64"], writes=[ps5k] + (["PS5"] if first else []))
            p.op("s", lambda s, dnb_ap=dnb_ap, ps5h=ps5h: s.copy(dnb_ap, ps5h),
                 reads=[ps5k], writes=[dnbk])
            p.op("v", lambda v, avb=avb, avn_ap=avn_ap, dnb_ap=dnb_ap: v.tensor_tensor(
                avn_ap, PS[avb][0:64, 0:QCH], dnb_ap,
                op=Alu.mult), reads=[f"PS{avb}", dnbk], writes=[avnk] + exa)
            p.dma("q", "stav",
                  xaT[h // 2][poff:poff + 64, qc * QCH:(qc + 1) * QCH],
                  avn_ap, reads=[avnk], writes=[f"xaT{h // 2}"])

        pend_denom = None
        dn_first = {0: True, 1: True}
        for h in range(HEADS):
            db = h % 2
            p.dma("q", f"ldd{db}", dstrip[db][:], dbias_d.ap()[h, :, 127:ZLEN],
                  reads=["dbias_dram"], writes=[f"dstrip{db}"])
            for qc in range(NQC):
                nbs = (qc * QCH + QCH - 1) // 128 + 1
                lag = 2 if OPT_AVLAG else 0
                for bs in range(nbs):
                    emit_score(h, qc, bs)
                    if bs >= lag:
                        emit_av(h, qc, bs - lag, nbs)
                    if bs == 2 and pend_denom is not None:
                        emit_denom(*pend_denom)
                        pend_denom = None
                for bs in range(max(0, nbs - lag), nbs):
                    emit_av(h, qc, bs, nbs)
                if pend_denom is not None:
                    emit_denom(*pend_denom)
                    pend_denom = None
                if OPT_DENOMLAG:
                    pend_denom = (h, qc)
                else:
                    emit_denom(h, qc)
            if b == 0 and OPT_PREPIL:
                for _ in range(2 if h % 8 < 4 else 3):
                    nt_ = next(ffnw_iter, None)
                    if nt_ is not None:
                        prep_tile(*nt_)
        if pend_denom is not None:
            emit_denom(*pend_denom)
            pend_denom = None
        if b == 0:
            while True:
                nt_ = next(ffnw_iter, None)
                if nt_ is None:
                    break
                prep_tile(*nt_)

        if dbg and b == 0:
            dq4 = nc.dram_tensor("dbg_ao0", [128, 512], BF16, kind="ExternalOutput").ap()
            p.dma("q", "stdbg", dq4, xaT[0][:, 0:512], reads=["xaT0"])
            dq5 = nc.dram_tensor("dbg_aoall", [8, 128, 64], BF16, kind="ExternalOutput").ap()
            for _i in range(8):
                p.dma("q", "stdbg", dq5[_i], xaT[_i][:, 0:64], reads=[f"xaT{_i}"])

        for ct in range(8):
            pjb = ct % 2
            extraw = X_A if ct == 0 else []
            p.dma("q", f"ldpj{pjb}", projbuf[pjb][:].rearrange("p (c n) -> p c n", c=8),
                  projT_d[:, :, ct * 128:(ct + 1) * 128].rearrange("c p n -> p c n"),
                  reads=["projT_dram"], writes=[f"projbuf{pjb}"] + extraw)
            for tc4 in range(NTC):
                pb = (ct + tc4) % 2
                for Ct in range(8):
                    p.op("t", lambda t, ct=ct, tc4=tc4, Ct=Ct, pb=pb, pjb=pjb: t.matmul(
                        PS[pb][:, 0:TC], projbuf[pjb][:, Ct * 128:(Ct + 1) * 128],
                        xaT[Ct][:, tc4 * TC:(tc4 + 1) * TC],
                        start=(Ct == 0), stop=(Ct == 7)),
                        reads=[f"projbuf{pjb}", f"xaT{Ct}"], writes=[f"PS{pb}"])
                sb_ = (ct + tc4) % 2
                p.op("v", lambda v, ct=ct, pb=pb, sb_=sb_: v.tensor_scalar(
                    pstage[sb_][:, 0:TC], PS[pb][:, 0:TC], projb_c[:, ct:ct + 1], None,
                    op0=Alu.add), reads=[f"PS{pb}", "projb_c"], writes=[f"pstage{sb_}"])
                p.dma("q", "stpp", ppart_d[b, ct, :, tc4 * TC:(tc4 + 1) * TC],
                      pstage[sb_][:, 0:TC], reads=[f"pstage{sb_}"], writes=["pp_dram"])

        for ct in range(8):
            extraw = (A_ATTN if ct == 0 else [])
            p.dma("q", "ldfw", wgq[ct][:], wgqT_d[ct, :, :], reads=["wgq_dram"],
                  writes=[f"wgq{ct}"] + extraw)
            p.dma("q", "ldfw", wuq[ct][:], wuqT_d[ct, :, :], reads=["wuq_dram"],
                  writes=[f"wuq{ct}"])
        def emit_hq_tr(tq):
            for it in range(NI):
                pb = [4, 5, 2, 3][it % 4]
                p.op("t", lambda t, it=it, pb=pb: t.matmul(
                    PS[pb][:, 0:128], wstage2[:, it * 128:(it + 1) * 128], ident_b[:],
                    start=True, stop=True),
                    reads=["wstage2", "ident_b"],
                    writes=[f"PS{pb}"] + (["PS5a", "PS5b"] if pb == 5 else []))
                p.op("s", lambda s, it=it, tq=tq, pb=pb: s.copy(
                    hqT[it][:, tq * 128:(tq + 1) * 128], PS[pb][:, 0:128]),
                    reads=[f"PS{pb}"], writes=[f"hqT{it}"])

        for tc4 in range(NTC):
            pend_tr = []
            for tq in range(NTQ):
                tt = tc4 * NTQ + tq
                xo = (tt % 2) * 8
                for ct in range(8):
                    extraw = B_ATTN if (tc4 == 0 and tq == 0 and ct == 0) else []
                    p.dma("q", "ldxq", xnq_s[xo + ct][:],
                          xnqT_d[b, ct, :, tt * 128:(tt + 1) * 128],
                          reads=["xnq_dram"], writes=[f"xnq_s{xo + ct}"] + extraw)
                for ic in range(NIC):
                    iw = min(512, FFI - ic * 512)
                    for ct in range(8):
                        p.op("t", lambda t, ct=ct, ic=ic, iw=iw, xo=xo: t.matmul(
                            PS[0][:, 0:iw], xnq_s[xo + ct][:],
                            wgq[ct][:, ic * 512:ic * 512 + iw],
                            start=(ct == 0), stop=(ct == 7)),
                            reads=[f"xnq_s{xo + ct}", f"wgq{ct}"], writes=["PS0"])
                    for ct in range(8):
                        p.op("t", lambda t, ct=ct, ic=ic, iw=iw, xo=xo: t.matmul(
                            PS[1][:, 0:iw], xnq_s[xo + ct][:],
                            wuq[ct][:, ic * 512:ic * 512 + iw],
                            start=(ct == 0), stop=(ct == 7)),
                            reads=[f"xnq_s{xo + ct}", f"wuq{ct}"], writes=["PS1"])
                    p.op("s", lambda s, iw=iw: s.activation(gs_b[:, 0:iw],
                         PS[0][:, 0:iw], Act.Silu), reads=["PS0"],
                         writes=["gs_b"] + (["dnr0"] if (tc4 == 0 and tq == 0
                                                         and ic == 0) else []))
                    p.op("v", lambda v, ic=ic, iw=iw: v.tensor_tensor(
                        fbuf0[:, ic * 512:ic * 512 + iw], gs_b[:, 0:iw],
                        PS[1][:, 0:iw], op=Alu.mult),
                        reads=["gs_b", "PS1"],
                        writes=["fbuf0"] + (["xtile0", "xtile1"]
                                            if (tc4 == 0 and tq == 0 and ic == 0) else []))
                while pend_tr:
                    emit_hq_tr(pend_tr.pop(0))
                quant("fbuf0", fbuf0[:, 0:FFI], FFI, wstage2[:, 0:FFI], "wstage2")
                pend_tr.append(tq)
            while pend_tr:
                emit_hq_tr(pend_tr.pop(0))
            for ct in range(8):
                wb_ = ct % 2
                exd = ["qdqw"] if (ct == 0 and tc4 == 0) else []
                p.dma("q", f"ldwd{wb_}", wdbuf[wb_][:].rearrange("p (c n) -> p c n", c=NI),
                      wdqT_d[:, :, ct * 128:(ct + 1) * 128].rearrange("c p n -> p c n"),
                      reads=["wdq_dram"], writes=[f"wdbuf{wb_}"] + exd)
                pb2 = 2 + (ct % 2)
                for it in range(NI):
                    p.op("t", lambda t, ct=ct, it=it, pb2=pb2, wb_=wb_: t.matmul(
                        PS[pb2][:, 0:TC], wdbuf[wb_][:, it * 128:(it + 1) * 128],
                        hqT[it][:, 0:TC], start=(it == 0), stop=(it == NI - 1)),
                        reads=[f"wdbuf{wb_}", f"hqT{it}"], writes=[f"PS{pb2}"])
                sb_ = ct % 2
                p.dma("q", f"ldpp{sb_}", ppbuf[sb_][:, 0:TC],
                      ppart_d[b, ct, :, tc4 * TC:(tc4 + 1) * TC],
                      reads=["pp_dram"], writes=[f"ppbuf{sb_}"])
                p.dma("q", f"ldxf{sb_}", xfbuf[sb_][:, 0:TC],
                      x_fm[b, ct * 128:(ct + 1) * 128, tc4 * TC:(tc4 + 1) * TC],
                      writes=[f"xfbuf{sb_}"])
                exo = (X_A + X_Q + X_P) if (tc4 == 0 and ct < 2) else []
                p.op("v", lambda v, sb_=sb_, pb2=pb2: v.tensor_tensor(
                    ostage[sb_][:, 0:TC], PS[pb2][:, 0:TC], ppbuf[sb_][:, 0:TC], op=Alu.add),
                    reads=[f"PS{pb2}", f"ppbuf{sb_}"], writes=[f"ostage{sb_}"] + exo)
                p.op("v", lambda v, sb_=sb_: v.tensor_tensor(
                    ostage[sb_][:, 0:TC], ostage[sb_][:, 0:TC], xfbuf[sb_][:, 0:TC], op=Alu.add),
                    reads=[f"xfbuf{sb_}"], writes=[f"ostage{sb_}"])
                p.dma("q", "stout",
                      out_fm[b, ct * 128:(ct + 1) * 128, tc4 * TC:(tc4 + 1) * TC],
                      ostage[sb_][:, 0:TC], reads=[f"ostage{sb_}"], writes=["out_dram"])

    if dbg:
        dbgout = {
            "dbg_wq": ([128, FQKV], BF16, wqkvT_d[0, :, :], "wqkvT_dram"),
            "dbg_db": ([128, ZLEN - 127], BF16, dbias_d.ap()[0, :, 127:ZLEN], "dbias_dram"),
            "dbg_xnq": ([128, T], BF16, xnqT_d[0, 0, :, :], "xnq_dram"),
            "dbg_pp": ([128, T], F32, ppart_d[0, 0, :, :], "pp_dram"),
            "dbg_xn": ([128, C], F32, xn[:, 0:C], "xn"),
            "dbg_ws2": ([128, FFI], BF16, wstage2[:, 0:FFI], "wstage2"),
            "dbg_bna": ([128, 2], F32, bna[:], "bna"),
            "dbg_aff": ([128, 4], F32, aff[:], "aff0"),
            "dbg_vt": ([128, FFI], F32, vt_[:], "vt"),
            "dbg_qnb": ([128, FFI], BF16, qnb[:], "qnb"),
            "dbg_sblk": ([128, FFI // 16], F32, sblk[:], "sblk"),
            "dbg_iblk": ([128, FFI // 16], F32, iblk[:], "iblk"),
            "dbg_mblk": ([128, FFI // 16], F32, mblk[:], "mblk"),
        }
        for nm, (shp, dt_, srcap, key) in dbgout.items():
            od = nc.dram_tensor(nm, shp, dt_, kind="ExternalOutput").ap()
            p.dma("q", "stdbg", od, srcap, reads=[key], writes=[])
        p.barrier("t", "stdbg")

    p.barrier("t", "stout")

    @blk.sync
    def _(e):
        p.emit("q", e)

    @blk.vector
    def _(e):
        p.emit("v", e)

    @blk.scalar
    def _(e):
        p.emit("s", e)

    @blk.tensor
    def _(e):
        p.emit("t", e)

    @blk.gpsimd
    def _(e):
        p.emit("g", e)

    p.ctx.close()
    if ret_p:
        return nc, p
    return nc


_CACHE = {}


def _get_nc(T, NB):
    key = (T, NB)
    if key not in _CACHE:
        _CACHE[key] = build_program(T, NB)
    return _CACHE[key]


def kernel(L0_attn_x, L0_attn_qkv_w, L0_attn_qkv_b, L0_attn_proj_w, L0_attn_proj_b,
           L0_attn_rel_pos_emb, L0_ffn_w_gate, L0_ffn_w_up, L0_ffn_w_down):
    x = np.ascontiguousarray(np.asarray(L0_attn_x, dtype=np.float32))
    B, T, _ = x.shape
    NCORES = 8
    NB = B // NCORES
    nc = _get_nc(T, NB)
    w = {
        "qkv_w": np.ascontiguousarray(np.asarray(L0_attn_qkv_w, dtype=np.float32)),
        "qkv_b": np.ascontiguousarray(np.asarray(L0_attn_qkv_b, dtype=np.float32)).reshape(1, -1),
        "proj_w": np.ascontiguousarray(np.asarray(L0_attn_proj_w, dtype=np.float32)),
        "proj_b": np.ascontiguousarray(np.asarray(L0_attn_proj_b, dtype=np.float32)).reshape(1, -1),
        "rel": np.ascontiguousarray(np.asarray(L0_attn_rel_pos_emb, dtype=np.float32)),
        "wg": np.ascontiguousarray(np.asarray(L0_ffn_w_gate, dtype=np.float32)),
        "wu": np.ascontiguousarray(np.asarray(L0_ffn_w_up, dtype=np.float32)),
        "wd": np.ascontiguousarray(np.asarray(L0_ffn_w_down, dtype=np.float32)),
    }
    in_maps = []
    for c in range(NCORES):
        xs = np.ascontiguousarray(x[c * NB:(c + 1) * NB])
        m = dict(w)
        m["x_tm"] = xs
        m["x_fm"] = np.ascontiguousarray(xs.transpose(0, 2, 1))
        in_maps.append(m)
    res = run_bass_kernel_spmd(nc, in_maps, core_ids=list(range(NCORES)))
    outs = [r["out_fm"].transpose(0, 2, 1) for r in res.results]
    return np.ascontiguousarray(np.concatenate(outs, axis=0)).astype(np.float32)


# version 12
# speedup vs baseline: 1.0017x; 1.0017x over previous
import sys
sys.path.insert(0, '/opt/trn_rl_repo')
import contextlib
import numpy as np
import concourse.bass as bass
import concourse.mybir as mybir
from concourse.bass_utils import run_bass_kernel_spmd

F32 = mybir.dt.float32
BF16 = mybir.dt.bfloat16
Alu = mybir.AluOpType
Act = mybir.ActivationFunctionType
AX = mybir.AxisListType.X

C = 1024
HEADS = 16
HD = 64
FQKV = 3 * C
FFI = 2816
MAXLEN = 1024
NI = FFI // 128
NTHR = [0.25, 0.75, 1.25, 1.75, 2.5, 3.5, 5.0]
NW = [0.5, 0.5, 0.5, 0.5, 1.0, 1.0, 2.0]
GPSIMD_QUANT = True
OPT_AVLAG = True
OPT_DENOMLAG = True
OPT_PREPIL = True


class Prog:

    def __init__(self, nc):
        self.nc = nc
        self.ctx = contextlib.ExitStack()
        self.eng = {"v": nc.vector, "s": nc.scalar, "t": nc.tensor,
                    "g": nc.gpsimd, "q": nc.sync}
        self.semobj = {}
        self.cnt = {}
        self.is_dma = {}
        for e in "vstg":
            self.semobj[e] = self.ctx.enter_context(nc.semaphore("sem_" + e))
            self.cnt[e] = 0
            self.is_dma[e] = False
        self.know = {e: {} for e in "vstgq"}
        self.lastw = {}
        self.readers = {}
        self.snap = {}
        self.queue = {e: [] for e in "vstgq"}
        self.log = []

    def dma_sem(self, name):
        if name not in self.semobj:
            self.semobj[name] = self.ctx.enter_context(self.nc.semaphore(name))
            self.cnt[name] = 0
            self.is_dma[name] = True
        return name

    def _waits(self, e, reads, writes, fw=False):
        need = {}
        for b in reads:
            t = self.lastw.get(b)
            if t is not None:
                need[t[0]] = max(need.get(t[0], 0), t[1])
        for b in writes:
            t = self.lastw.get(b)
            if t is not None:
                need[t[0]] = max(need.get(t[0], 0), t[1])
            for rt in self.readers.get(b, ()):
                need[rt[0]] = max(need.get(rt[0], 0), rt[1])
        kn = self.know[e]
        for sname in sorted(need):
            val = need[sname]
            if sname == e and not fw and not getattr(self, 'FW_ALL', False):
                continue
            if self.is_dma[sname]:
                val = self.cnt[sname]
            if kn.get(sname, 0) >= val:
                continue
            mul = 16 if self.is_dma[sname] else 1
            self.queue[e].append(("w", sname, val * mul))
            sn = self.snap.get((sname, val))
            if sn:
                for k2, v2 in sn.items():
                    if kn.get(k2, 0) < v2:
                        kn[k2] = v2
            kn[sname] = max(kn.get(sname, 0), val)

    def _finish(self, e, semname, reads, writes, inc):
        self.cnt[semname] += 1
        tag = (semname, self.cnt[semname])
        sn = dict(self.know[e])
        sn[semname] = self.cnt[semname]
        if e in "vstg":
            sn[e] = self.cnt.get(e, 0)
        self.snap[tag] = sn
        for b in writes:
            self.lastw[b] = tag
            self.readers[b] = []
        for b in reads:
            self.readers.setdefault(b, []).append(tag)

    def op(self, e, f, reads=(), writes=(), fw=False):
        assert e in "vstg"
        n0 = len(self.queue[e])
        self._waits(e, reads, writes, fw=fw)
        self.log.append((e, "op", tuple(reads), tuple(writes),
                         tuple(self.queue[e][n0:])))
        sem = self.semobj[e]
        self.queue[e].append(("i", f, sem, 1))
        self._finish(e, e, reads, writes, 1)

    def dma(self, e, semname, out, in_, reads=(), writes=()):
        self.dma_sem(semname)
        n0 = len(self.queue[e])
        self._waits(e, reads, writes)
        self.log.append((e, "dma:" + semname, tuple(reads), tuple(writes),
                         tuple(self.queue[e][n0:])))
        sem = self.semobj[semname]
        self.queue[e].append(("d", out, in_, sem))
        self._finish(e, semname, reads, writes, 16)

    def barrier(self, e, semname):
        val = self.cnt.get(semname, 0)
        if val == 0:
            return
        kn = self.know[e]
        if kn.get(semname, 0) >= val:
            return
        mul = 16 if self.is_dma[semname] else 1
        self.queue[e].append(("w", semname, val * mul))
        kn[semname] = val

    def emit(self, e, engine):
        for item in self.queue[e]:
            if item[0] == "w":
                engine.wait_ge(self.semobj[item[1]], item[2])
            elif item[0] == "i":
                inst = item[1](engine)
                inst.then_inc(item[2], item[3])
            else:
                inst = engine.dma_start(out=item[1], in_=item[2])
                inst.then_inc(item[3], 16)


def build_program(T=2048, NB=2, dbg=False, ret_p=False):
    QCH = min(512, T)
    TC = QCH
    NT = T // 128
    NTC = T // TC
    NTQ = TC // 128
    NQC = T // QCH
    ZOFF = QCH - 1
    ZLEN = T + ZOFF
    DW = 127 + ZLEN
    NIC = (FFI + 511) // 512
    TW2 = T * 2 // 4
    assert T % 512 == 0 or T == QCH

    nc = bass.Bass("TRN2", target_bir_lowering=False, detect_race_conditions=False)
    p = Prog(nc)

    x_tm = nc.dram_tensor("x_tm", [NB, T, C], F32, kind="ExternalInput").ap()
    x_fm = nc.dram_tensor("x_fm", [NB, C, T], F32, kind="ExternalInput").ap()
    qkv_w = nc.dram_tensor("qkv_w", [FQKV, C], F32, kind="ExternalInput").ap()
    qkv_b = nc.dram_tensor("qkv_b", [1, FQKV], F32, kind="ExternalInput").ap()
    proj_w = nc.dram_tensor("proj_w", [C, C], F32, kind="ExternalInput").ap()
    proj_b = nc.dram_tensor("proj_b", [1, C], F32, kind="ExternalInput").ap()
    rel = nc.dram_tensor("rel", [2 * MAXLEN - 1, C], F32, kind="ExternalInput").ap()
    wg_d = nc.dram_tensor("wg", [FFI, C], F32, kind="ExternalInput").ap()
    wu_d = nc.dram_tensor("wu", [FFI, C], F32, kind="ExternalInput").ap()
    wd_d = nc.dram_tensor("wd", [C, FFI], F32, kind="ExternalInput").ap()
    out_fm = nc.dram_tensor("out_fm", [NB, C, T], F32, kind="ExternalOutput").ap()

    wqkvT_d = nc.dram_tensor("wqkvT_d", [8, 128, FQKV], BF16, kind="Internal").ap()
    projT_d = nc.dram_tensor("projT_d", [8, 128, C], BF16, kind="Internal").ap()
    wgqT_d = nc.dram_tensor("wgqT_d", [8, 128, FFI], BF16, kind="Internal").ap()
    wuqT_d = nc.dram_tensor("wuqT_d", [8, 128, FFI], BF16, kind="Internal").ap()
    wdqT_d = nc.dram_tensor("wdqT_d", [NI, 128, C], BF16, kind="Internal").ap()
    tcf_d = nc.dram_tensor("tcf_d", [HEADS, ZLEN], F32, kind="Internal")
    dbias_d = nc.dram_tensor("dbias_d", [HEADS, 128, DW], BF16, kind="Internal")
    xnqT_d = nc.dram_tensor("xnqT_d", [NB, 8, 128, T], BF16, kind="Internal").ap()
    ppart_d = nc.dram_tensor("ppart_d", [NB, 8, 128, T], F32, kind="Internal").ap()

    MTOTAL = [0]

    def mk(name, shape, dtype, off, seg=0):
        nbytes = shape[1] * (2 if dtype == BF16 else 4)
        start = MBASE + seg + off
        MTOTAL[0] = max(MTOTAL[0], seg + off + nbytes)
        return nc.alloc_sbuf_tensor_at(name, list(shape), dtype, offset=start)

    qkT_b0 = 4096 if T >= 2048 else T * 2
    _A_sz = max(16 * qkT_b0 + 16 * 2112, 16 * 5632, 10240 + 8192)
    _SW = ZLEN - 127
    _Xoff = 8 * qkT_b0 + 2 * _SW * 2
    _B_sz = max(_Xoff + 8192, 22 * 1024 + 4096 + 11264 + 3 * 4096)
    _TOT = _A_sz + _B_sz + 4096 + 28672 + 24576
    MBASE = (nc.sbuf_top - _TOT) // 32 * 32

    SA = 0
    qkT_b = qkT_b0
    A_attn = 16 * qkT_b + 16 * 2112
    A_ffn = 16 * 5632
    A_sz = max(A_attn, A_ffn, 10240 + 8192)
    qkT = [mk(f"qkT{i}", [128, T], BF16, i * qkT_b, SA) for i in range(16)]
    vaug = [mk(f"vaug{i}", [128, HEADS * 65], BF16, 16 * qkT_b + i * 2112, SA)
            for i in range(NT)]
    wgq = [mk(f"wgq{i}", [128, FFI], BF16, i * 5632, SA) for i in range(8)]
    wuq = [mk(f"wuq{i}", [128, FFI], BF16, 45056 if False else 8 * 5632 + i * 5632, SA)
           for i in range(8)] if A_sz >= 16 * 5632 else None
    if wuq is None:
        wuq = [mk(f"wuq{i}", [128, FFI], BF16, 8 * 5632 + i * 5632, SA) for i in range(8)]
    tcf = mk("tcf", [128, ZLEN], F32, 0, SA)
    tblT = mk("tblT", [128, 2048], F32, ZLEN * 4 + 32 - (ZLEN * 4) % 32, SA)
    A_P0 = ["tblT", "tcf"]
    A_ATTN = [f"qkT{i}" for i in range(16)] + [f"vaug{i}" for i in range(NT)]
    A_FFN = [f"wgq{i}" for i in range(8)] + [f"wuq{i}" for i in range(8)]

    SB = SA + A_sz
    SW = ZLEN - 127
    xaT_b = 4096 if T >= 2048 else T * 2
    ds_b = SW * 2
    Xoff = 8 * xaT_b + 2 * ds_b
    B_attn = Xoff + 8192
    B_ffn = 22 * 1024 + 4096 + 11264 + 3 * 4096
    B_sz = max(B_attn, B_ffn)
    xaT = [mk(f"xaT{i}", [128, T], BF16, i * xaT_b, SB) for i in range(8)]
    dstrip = [mk(f"dstrip{i}", [128, SW], BF16, 8 * xaT_b + i * ds_b, SB) for i in range(2)]
    wqbuf = [mk(f"wqbuf{i}", [128, 1024], BF16, Xoff + i * 2048, SB) for i in range(2)]
    vwbuf = [mk(f"vwbuf{i}", [128, 512], BF16, Xoff + i * 1024, SB) for i in range(8)]
    attT = [mk(f"attT{i}", [128, QCH], BF16, Xoff + i * 1024, SB) for i in range(3)]
    exps = [mk(f"exps{i}", [128, QCH], BF16, Xoff + 4928 + i * 1024, SB) for i in range(2)]
    avn = mk("avn", [128, QCH], BF16, Xoff + 3 * 1024, SB)
    avn2 = mk("avn2", [128, QCH], BF16, Xoff + 6976, SB)
    projbuf = [mk(f"projbuf{i}", [128, 1024], BF16, Xoff + i * 2048, SB) for i in range(2)]
    ident_f = mk("ident_f", [128, 128], F32, Xoff + 4096, SB)
    onesb = mk("onesb", [128, 128], BF16, Xoff + 4608, SB)
    tstage = mk("tstage", [128, 16], F32, Xoff + 4864, SB)
    hqT = [mk(f"hqT{i}", [128, 512], BF16, i * 1024, SB) for i in range(NI)]
    xnq_s = [mk(f"xnq_s{i}", [128, 128], BF16, 22 * 1024 + i * 256, SB) for i in range(16)]
    wdbuf = [mk(f"wdbuf{i}", [128, FFI], BF16, 26624 + i * 5632, SB) for i in range(2)]
    ppbuf = [mk(f"ppbuf{i}", [128, 512], F32, 37888 + i * 2048, SB) for i in range(2)]
    xfbuf = [mk(f"xfbuf{i}", [128, 512], F32, 41984 + i * 2048, SB) for i in range(2)]
    ostage = [mk(f"ostage{i}", [128, 512], F32, 46080 + i * 2048, SB) for i in range(2)]
    qdqw = mk("qdqw", [128, FFI], BF16, 26624, SB)
    X_Q = ["wqbuf0", "wqbuf1"] + [f"vwbuf{i}" for i in range(8)]
    X_A = ["attT0", "attT1", "attT2", "avn", "avn2", "attf0", "attf1",
           "exps0", "exps1"]
    X_P = ["pstage0", "pstage1", "projbuf0", "projbuf1"]
    B_ATTN = ([f"xaT{i}" for i in range(8)] + ["dstrip0", "dstrip1"] + X_Q + X_A + X_P
              + ["qdqw", "ident_f", "onesb", "tstage"])
    B_FFN = ([f"hqT{i}" for i in range(NI)] + ["ppbuf0", "ppbuf1", "xfbuf0", "xfbuf1",
              "ostage0", "ostage1", "wdbuf0", "wdbuf1"] + [f"xnq_s{i}" for i in range(16)])

    SC = SB + B_sz
    attf = [mk(f"attf{i}", [128, QCH], F32, i * 2048, SC) for i in range(2)]
    pstage = [mk(f"pstage{i}", [128, 512], F32, i * 2048, SC) for i in range(2)]
    B32_sz = 4096

    SD = SC + B32_sz
    vt_ = mk("vt", [128, FFI], F32, 0, SD)
    fbuf0 = mk("fbuf0", [128, FFI], F32, 11264, SD)
    fcast = mk("fcast", [128, FFI], BF16, 11264, SD)
    fcast2 = mk("fcast2", [128, FFI], BF16, 16896, SD)
    xtile = [mk(f"xtile{i}", [128, C], F32, 11264 + i * 4096, SD) for i in range(2)]
    gs_b = mk("gs_b", [128, 512], F32, 22528, SD)
    xn = mk("xn", [128, C], F32, 24576, SD)
    C32_sz = 28672
    dnr = gs_b
    dnb = xn

    SE = SD + C32_sz
    qnb = mk("qnb", [128, FFI], BF16, 0, SE)
    cmb = mk("cmb", [128, FFI], BF16, 5632, SE)
    sgn_b = mk("sgn", [128, FFI], BF16, 11264, SE)
    wstage2 = mk("wstage2", [128, FFI], BF16, 16896, SE)
    wstage = mk("wstage", [128, 1024], BF16, 16896, SE)
    wstage3 = mk("wstage3", [128, 1024], BF16, 18944, SE)
    tcb16 = mk("tcb16", [128, ZLEN], BF16, 16896, SE)
    xnb = mk("xnb", [128, C], BF16, 22528, SE)
    qdq = mk("qdq", [128, C], BF16, 22528, SE)
    C16_sz = 24576

    mblk = p.ctx.enter_context(nc.sbuf_tensor("mblk", [128, FFI // 16], F32))
    sblk = p.ctx.enter_context(nc.sbuf_tensor("sblk", [128, FFI // 16], F32))
    iblk = p.ctx.enter_context(nc.sbuf_tensor("iblk", [128, FFI // 16], F32))
    bns = p.ctx.enter_context(nc.sbuf_tensor("bns", [128, 12], F32))
    bna = p.ctx.enter_context(nc.sbuf_tensor("bna", [128, 2], F32))
    aff = p.ctx.enter_context(nc.sbuf_tensor("aff", [128, 4], F32))
    qkvb_c = p.ctx.enter_context(nc.sbuf_tensor("qkvb_c", [128, 24], F32))
    projb_c = p.ctx.enter_context(nc.sbuf_tensor("projb_c", [128, 8], F32))
    vbb = p.ctx.enter_context(nc.sbuf_tensor("vbb", [128, C], BF16))
    ident_b = p.ctx.enter_context(nc.sbuf_tensor("ident_b", [128, 128], BF16))
    ones64 = p.ctx.enter_context(nc.sbuf_tensor("ones64", [128, 64], F32))

    PS = [p.ctx.enter_context(nc.psum_tensor(f"ps{i}", [128, 512], F32)) for i in range(6)]
    PS6B = p.ctx.enter_context(nc.psum_tensor("ps6b", [128, 1024], BF16))
    PS7B = p.ctx.enter_context(nc.psum_tensor("ps7b", [128, 1024], BF16))
    PS = PS + [PS6B, PS7B]
    PSB = [None] * 6 + [PS6B[:, 0:128], PS7B[:, 0:128]]

    blk = p.ctx.enter_context(nc.Block())

    p.op("g", lambda g: g.memset(onesb[:], 1.0), writes=["onesb"])
    p.op("g", lambda g: g.affine_select(ident_b[:], onesb[:], pattern=[[-1, 128]],
         compare_op=Alu.is_equal, fill=0.0, base=0, channel_multiplier=1),
         reads=["onesb"], writes=["ident_b"])
    p.op("g", lambda g: g.affine_select(ident_f[:], onesb[:], pattern=[[-1, 128]],
         compare_op=Alu.is_equal, fill=0.0, base=0, channel_multiplier=1),
         reads=["onesb"], writes=["ident_f"])
    p.op("v", lambda v: v.memset(ones64[:], 1.0), writes=["ones64"])

    qb_src = bass.AP(tensor=qkv_b.tensor, offset=0, ap=[[128, 24], [1, 128]])
    p.dma("q", "ld0", gs_b[0:24, 0:128], qb_src, writes=["gs_b"])
    p.op("t", lambda t: t.transpose(PS[5][:, 0:24], gs_b[0:24, 0:128], ident_f[0:24, 0:24]),
         reads=["gs_b", "ident_f"], writes=["PS5"])
    p.op("s", lambda s: s.copy(qkvb_c[:], PS[5][:, 0:24]), reads=["PS5"], writes=["qkvb_c"])
    pb_src = bass.AP(tensor=proj_b.tensor, offset=0, ap=[[128, 8], [1, 128]])
    p.dma("q", "ld0", gs_b[0:8, 128:256], pb_src, writes=["gs_b"])
    p.op("t", lambda t: t.transpose(PS[5][:, 0:8], gs_b[0:8, 128:256], ident_f[0:8, 0:8]),
         reads=["gs_b", "ident_f"], writes=["PS5"])
    p.op("s", lambda s: s.copy(projb_c[:], PS[5][:, 0:8]), reads=["PS5"], writes=["projb_c"])
    p.dma("g", "ldvb", vbb[:], qkv_b[0, 2 * C:3 * C].unsqueeze(0).broadcast_to([128, C]),
          writes=["vbb"])

    for wt in range(24 + 8):
        is_proj = wt >= 24
        src = proj_w if is_proj else qkv_w
        row = (wt - 24 if is_proj else wt) * 128
        fcb, fck = (fcast, "fcast") if wt % 2 == 0 else (fcast2, "fcast2")
        wsb, wsk = (wstage, "wstage") if wt % 2 == 0 else (wstage3, "wstage3")
        p.dma("g", "ldwg", fcb[:, 0:C], src[row:row + 128, :], writes=[fck])
        for ct in range(8):
            pb = ct % 6
            p.op("t", lambda t, ct=ct, pb=pb, fcb=fcb: t.matmul(
                PS[pb][:, 0:128], fcb[:, ct * 128:(ct + 1) * 128], ident_b[:],
                start=True, stop=True),
                reads=[fck, "ident_b"], writes=[f"PS{pb}"])
            p.op("s", lambda s, ct=ct, pb=pb, wsb=wsb: s.copy(
                wsb[:, ct * 128:(ct + 1) * 128], PS[pb][:, 0:128]),
                reads=[f"PS{pb}"], writes=[wsk])
        dstd = projT_d if is_proj else wqkvT_d
        dstk = "projT_dram" if is_proj else "wqkvT_dram"
        p.dma("q", "stw", dstd[:, :, row:row + 128].rearrange("c p n -> p c n"),
              wsb[:].rearrange("p (c n) -> p c n", c=8),
              reads=[wsk], writes=[dstk])

    nrt = (2 * MAXLEN - 1 + 127) // 128
    for rt in range(nrt):
        row0 = min(rt * 128, 2 * MAXLEN - 1 - 128)
        p.dma("q", "ldr", fbuf0[:, 0:C], rel[row0:row0 + 128, :],
              writes=["fbuf0"] + (["fcast", "fcast2"] if row0 == 0 else []))
        p.op("v", lambda v: v.tensor_reduce(
            tstage[:, 0:16], fbuf0[:, 0:C].rearrange("p (h d) -> p h d", d=HD),
            axis=AX, op=Alu.add), reads=["fbuf0"], writes=["tstage"])
        p.op("t", lambda t: t.transpose(PS[5][0:16, 0:128], tstage[:, 0:16], ident_f[:]),
             reads=["tstage", "ident_f"], writes=["PS5"])
        p.op("s", lambda s, row0=row0: s.copy(tblT[0:16, row0:row0 + 128],
                                              PS[5][0:16, 0:128]),
             reads=["PS5"], writes=["tblT"])
    p.op("v", lambda v: v.memset(tcf[0:16, 0:ZOFF], -30000.0), writes=["tcf"])
    biaslen = min(1022, T - 1) + 1
    p.op("v", lambda v: v.tensor_copy(tcf[0:16, ZOFF:ZOFF + biaslen],
                                      tblT[0:16, 1023:1023 + biaslen]),
         reads=["tblT"], writes=["tcf"])
    if ZOFF + biaslen < ZLEN:
        fl = ZLEN - ZOFF - biaslen
        p.op("v", lambda v, fl=fl: v.tensor_scalar(
            tcf[0:16, ZOFF + biaslen:ZLEN], tcf[0:16, 0:fl], 0.0,
            tblT[0:16, 2046:2047], op0=Alu.mult, op1=Alu.add),
            reads=["tblT", "tcf"], writes=["tcf"])
    p.op("s", lambda s: s.activation(tcf[0:16, 0:ZLEN], tcf[0:16, 0:ZLEN], Act.Exp),
         reads=["tcf"], writes=["tcf"], fw=True)
    p.dma("q", "sttc", tcf_d.ap()[:, :], tcf[0:16, 0:ZLEN], reads=["tcf"],
          writes=["tcf_dram"])
    for h in range(HEADS):
        p.dma("g", "ldtc", tcb16[:],
              tcf_d.ap()[h, :].unsqueeze(0).broadcast_to([128, ZLEN]),
              reads=["tcf_dram"],
              writes=["wstage2"] + (["wstage", "wstage3"] if h == 0 else []))
        ddst = bass.AP(tensor=dbias_d, offset=h * 128 * DW, ap=[[DW + 1, 128], [1, ZLEN]])
        p.dma("q", "std", ddst, tcb16[:], reads=["wstage2"], writes=["dbias_dram"])

    def quant(src_key, src_ap, width, dst_ap, dst_key, extra_reads=()):
        er = list(extra_reads)
        nch = 2 if width > 1024 else 1
        cw = width // nch
        first = True
        for c in range(nch):
            nb = cw // 16
            b0 = c * (cw // 16)
            x3 = src_ap[:, c * cw:(c + 1) * cw].rearrange("p (b k) -> p b k", k=16)
            sl = (slice(None), slice(c * cw, (c + 1) * cw))
            mb = mblk[:, b0:b0 + nb]
            sb_ = sblk[:, b0:b0 + nb]
            ibl = iblk[:, b0:b0 + nb]
            mk_, sk_, ik_ = f"mblk{c}", f"sblk{c}", f"iblk{c}"
            p.op("v", lambda v, x3=x3, mb=mb, nb=nb: v.tensor_reduce(
                mb, x3, axis=AX, op=Alu.max, apply_absolute_value=True),
                reads=[src_key] + (er if first else []), writes=[mk_])
            p.op("v", lambda v, mb=mb, sb_=sb_: v.tensor_scalar(
                sb_, mb, 1.0 / 6.0, 1e-12, op0=Alu.mult, op1=Alu.max),
                reads=[mk_], writes=[sk_])
            p.op("v", lambda v, sb_=sb_, ibl=ibl: v.reciprocal(ibl, sb_),
                 reads=[sk_], writes=[ik_])
            ib3 = ibl.unsqueeze(2).broadcast_to([128, nb, 16])
            sb3 = sb_.unsqueeze(2).broadcast_to([128, nb, 16])
            vtc = vt_[sl]
            p.op("g", lambda g, x3=x3, ib3=ib3, vtc=vtc: g.tensor_tensor(
                vtc.rearrange("p (b k) -> p b k", k=16), x3, ib3, op=Alu.mult),
                reads=[src_key, ik_] + (er if first else []), writes=[f"vt{c}"])
            p.op("v", lambda v, vtc=vtc, c=c, cw=cw: v.tensor_scalar(
                cmb[:, c * cw:(c + 1) * cw], vtc, 0.0, 2.0,
                op0=Alu.is_gt, op1=Alu.mult), reads=[f"vt{c}"], writes=[f"cmb{c}"])
            p.op("v", lambda v, c=c, cw=cw: v.tensor_scalar(
                sgn_b[:, c * cw:(c + 1) * cw], cmb[:, c * cw:(c + 1) * cw], -1.0,
                None, op0=Alu.add), reads=[f"cmb{c}"], writes=[f"sgn{c}"])
            p.op("s", lambda s, vtc=vtc: s.square(vtc, vtc),
                 reads=[f"vt{c}"], writes=[f"vt{c}"], fw=False)
            p.op("v", lambda v, vtc=vtc, c=c, cw=cw: v.tensor_scalar(
                qnb[:, c * cw:(c + 1) * cw], vtc, NTHR[0] ** 2, NW[0],
                op0=Alu.is_gt, op1=Alu.mult), reads=[f"vt{c}"], writes=[f"qnb{c}"])
            for i in range(1, 7):
                p.op("v", lambda v, i=i, vtc=vtc, c=c, cw=cw: v.tensor_scalar(
                    cmb[:, c * cw:(c + 1) * cw], vtc, NTHR[i] ** 2, NW[i],
                    op0=Alu.is_gt, op1=Alu.mult), reads=[f"vt{c}"], writes=[f"cmb{c}"])
                p.op("v", lambda v, c=c, cw=cw: v.tensor_tensor(
                    qnb[:, c * cw:(c + 1) * cw], qnb[:, c * cw:(c + 1) * cw],
                    cmb[:, c * cw:(c + 1) * cw], op=Alu.add),
                    reads=[f"cmb{c}"], writes=[f"qnb{c}"])
            p.op("v", lambda v, c=c, cw=cw: v.tensor_tensor(
                qnb[:, c * cw:(c + 1) * cw], qnb[:, c * cw:(c + 1) * cw],
                sgn_b[:, c * cw:(c + 1) * cw], op=Alu.mult),
                reads=[f"sgn{c}"], writes=[f"qnb{c}"])
            p.op("g", lambda g, c=c, cw=cw, sb3=sb3, nb=nb: g.tensor_tensor(
                dst_ap[:, c * cw:(c + 1) * cw].rearrange("p (b k) -> p b k", k=16),
                qnb[:, c * cw:(c + 1) * cw].rearrange("p (b k) -> p b k", k=16),
                sb3, op=Alu.mult),
                reads=[f"qnb{c}", sk_],
                writes=[dst_key] + (["xnb"] if dst_key == "qdq" and first else []))
            first = False

    def prep_tile(src_dram, wt, width, nct, dst_dram, dkey, p0=False):
        st = wstage if nct == 8 else wstage2
        stk = "wstage" if nct == 8 else "wstage2"
        dst_q = qdqw if width == FFI else qdq
        dqk = "qdqw" if width == FFI else "qdq"
        p.dma("q", "ldf", fbuf0[:, 0:width], src_dram[wt * 128:(wt + 1) * 128, :],
              writes=["fbuf0"])
        quant("fbuf0", fbuf0[:, 0:width], width, dst_q[:, 0:width], dqk)
        for ct in range(nct):
            if p0:
                pb = ct % 6
                p.op("t", lambda t, ct=ct, pb=pb, dst_q=dst_q: t.matmul(
                    PS[pb][:, 0:128], dst_q[:, ct * 128:(ct + 1) * 128], ident_b[:],
                    start=True, stop=True),
                    reads=[dqk, "ident_b"], writes=[f"PS{pb}"])
                p.op("s", lambda s, ct=ct, pb=pb, st=st: s.copy(
                    st[:, ct * 128:(ct + 1) * 128], PS[pb][:, 0:128]),
                    reads=[f"PS{pb}"], writes=[stk])
                continue
            pb = 6 + (ct % 2)
            p.op("t", lambda t, ct=ct, pb=pb, dst_q=dst_q: t.transpose(
                PSB[pb], dst_q[:, ct * 128:(ct + 1) * 128], ident_b[:]),
                reads=[dqk, "ident_b"], writes=[f"PS{pb}"])
            p.op("s", lambda s, ct=ct, pb=pb, st=st: s.copy(
                st[:, ct * 128:(ct + 1) * 128], PSB[pb]),
                reads=[f"PS{pb}"], writes=[stk])
        p.dma("q", "stw", dst_dram[:, :, wt * 128:(wt + 1) * 128].rearrange("c p n -> p c n"),
              st[:, 0:nct * 128].rearrange("p (c n) -> p c n", c=nct),
              reads=[stk], writes=[dkey])

    for wt in range(8):
        prep_tile(wd_d, wt, FFI, NI, wdqT_d, "wdq_dram", p0=True)
    ffnw_tiles = ([(wg_d, wt, C, 8, wgqT_d, "wgq_dram") for wt in range(NI)]
                  + [(wu_d, wt, C, 8, wuqT_d, "wuq_dram") for wt in range(NI)])
    ffnw_iter = iter(ffnw_tiles)

    for b in range(NB):
        for tt in range(NT):
            fb = tt % 2
            if tt == 0:
                extraw = (A_FFN + B_FFN) if b > 0 else ["qdqw"]
            else:
                extraw = []
            p.dma("q", f"ldx{fb}", xtile[fb][:], x_tm[b, tt * 128:(tt + 1) * 128, :],
                  writes=[f"xtile{fb}"] + (["fbuf0"] if tt < 2 else []))
            p.op("v", lambda v, fb=fb: v.bn_stats(bns[:, 0:6], xtile[fb][:, 0:512]),
                 reads=[f"xtile{fb}"], writes=["bns"])
            p.op("v", lambda v, fb=fb: v.bn_stats(bns[:, 6:12], xtile[fb][:, 512:1024]),
                 reads=[f"xtile{fb}"], writes=["bns"])
            p.op("v", lambda v: v.bn_aggr(bna[:], bns[:]), reads=["bns"], writes=["bna"],
                 fw=True)
            p.op("v", lambda v: v.tensor_scalar(aff[:, 2:3], bna[:, 1:2], 1e-5, None,
                 op0=Alu.add), reads=["bna"], writes=["aff2"], fw=True)
            p.op("s", lambda s: s.sqrt(aff[:, 3:4], aff[:, 2:3]), reads=["aff2"],
                 writes=["aff3"])
            p.op("v", lambda v: v.reciprocal(aff[:, 0:1], aff[:, 3:4]),
                 reads=["aff3"], writes=["aff0"], fw=True)
            p.op("v", lambda v: v.tensor_tensor(aff[:, 1:2], bna[:, 0:1], aff[:, 0:1],
                 op=Alu.mult), reads=["bna", "aff0"], writes=["aff1"], fw=True)
            p.op("v", lambda v: v.tensor_scalar(aff[:, 1:2], aff[:, 1:2], -1.0, None,
                 op0=Alu.mult), reads=["aff1"], writes=["aff1"], fw=True)
            p.op("v", lambda v, fb=fb: v.tensor_scalar(xn[:, 0:C], xtile[fb][:],
                 aff[:, 0:1], aff[:, 1:2], op0=Alu.mult, op1=Alu.add),
                 reads=[f"xtile{fb}", "aff0", "aff1"],
                 writes=["xn"] + (["dnb0", "dnb1", "dnr1"] if tt == 0 else []),
                 fw=True)
            p.op("v", lambda v: v.tensor_copy(xnb[:, 0:C], xn[:, 0:C]),
                 reads=["xn"], writes=["xnb"])
            for ct in range(8):
                pb = ct % 6
                p.op("t", lambda t, ct=ct, pb=pb: t.matmul(
                    PS[pb][:, 0:128], xnb[:, ct * 128:(ct + 1) * 128], ident_b[:],
                    start=True, stop=True),
                    reads=["xnb", "ident_b"],
                    writes=[f"PS{pb}"] + (["PS5a", "PS5b"] if pb == 5 else []))
                p.op("s", lambda s, ct=ct, tt=tt, pb=pb: s.copy(
                    xaT[ct][:, tt * 128:(tt + 1) * 128], PS[pb][:, 0:128]),
                    reads=[f"PS{pb}"], writes=[f"xaT{ct}"] + (extraw if ct == 0 else []))
            quant("xn", xn[:, 0:C], C, qdq[:, 0:C], "qdq")
            for ct in range(8):
                pb = (ct + 2) % 6
                p.op("t", lambda t, ct=ct, pb=pb: t.matmul(
                    PS[pb][:, 0:128], qdq[:, ct * 128:(ct + 1) * 128], ident_b[:],
                    start=True, stop=True),
                    reads=["qdq", "ident_b"],
                    writes=[f"PS{pb}"] + (["PS5a", "PS5b"] if pb == 5 else []))
                p.op("s", lambda s, ct=ct, pb=pb: s.copy(
                    wstage[:, ct * 128:(ct + 1) * 128], PS[pb][:, 0:128]),
                    reads=[f"PS{pb}"], writes=["wstage"])
            p.dma("q", "stxq", xnqT_d[b, :, :, tt * 128:(tt + 1) * 128].rearrange("c p n -> p c n"),
                  wstage[:].rearrange("p (c n) -> p c n", c=8),
                  reads=["wstage"], writes=["xnq_dram"])

        for ft in range(16):
            fb = ft % 2
            extraw = (X_A + X_P + [f"vwbuf{i}" for i in range(8)]) if (ft == 0 and b > 0) else []
            p.dma("q", f"ldq{fb}",
                  wqbuf[fb][:].rearrange("p (c n) -> p c n", c=8),
                  wqkvT_d[:, :, ft * 128:(ft + 1) * 128].rearrange("c p n -> p c n"),
                  reads=["wqkvT_dram"],
                  writes=[f"wqbuf{fb}"] + extraw)
            for tc4 in range(NTC):
                pb = (ft + tc4) % 2
                for ct in range(8):
                    p.op("t", lambda t, ft=ft, tc4=tc4, ct=ct, fb=fb, pb=pb: t.matmul(
                        PS[pb][:, 0:TC], wqbuf[fb][:, ct * 128:(ct + 1) * 128],
                        xaT[ct][:, tc4 * TC:(tc4 + 1) * TC],
                        start=(ct == 0), stop=(ct == 7)),
                        reads=[f"wqbuf{fb}", f"xaT{ct}"], writes=[f"PS{pb}"])
                is_q = ft < 8
                exw = [f"qkT{ft}"]
                if tc4 == 0 and ft == 0:
                    exw = exw + (A_FFN if b > 0 else A_P0)
                p.op("v", lambda v, ft=ft, tc4=tc4, pb=pb, is_q=is_q: v.tensor_scalar(
                    qkT[ft][:, tc4 * TC:(tc4 + 1) * TC], PS[pb][:, 0:TC],
                    qkvb_c[:, ft:ft + 1], (0.125 if is_q else 1.0),
                    op0=Alu.add, op1=Alu.mult),
                    reads=[f"PS{pb}", "qkvb_c"], writes=exw)
        for tt in range(NT):
            p.op("v", lambda v, tt=tt: v.memset(vaug[tt][:], 1.0), writes=[f"vaug{tt}"])
        for fc in range(2):
            for ct in range(8):
                exv = (["wqbuf0", "wqbuf1", "ident_f", "onesb", "tstage"]
                       if (fc == 0 and ct == 0) else [])
                p.dma("q", "ldvw", vwbuf[ct][:],
                      wqkvT_d[ct, :, 2 * C + fc * 512:2 * C + (fc + 1) * 512],
                      reads=["wqkvT_dram"], writes=[f"vwbuf{ct}"] + exv)
            for tt in range(NT):
                pb = tt % 2
                for ct in range(8):
                    p.op("t", lambda t, tt=tt, ct=ct, pb=pb: t.matmul(
                        PS[pb][:, 0:512], xaT[ct][:, tt * 128:(tt + 1) * 128],
                        vwbuf[ct][:], start=(ct == 0), stop=(ct == 7)),
                        reads=[f"vwbuf{ct}", f"xaT{ct}"], writes=[f"PS{pb}"])
                p.op("v", lambda v, tt=tt, fc=fc, pb=pb: v.tensor_tensor(
                    vaug[tt][:].rearrange("p (h e) -> p h e", e=65)[:, fc * 8:(fc + 1) * 8, 0:64],
                    PS[pb][:, 0:512].rearrange("p (h d) -> p h d", d=64),
                    vbb[:, fc * 512:(fc + 1) * 512].rearrange("p (h d) -> p h d", d=64),
                    op=Alu.add), reads=[f"PS{pb}", "vbb"], writes=[f"vaug{tt}"])

        if dbg and b == 0:
            dq1 = nc.dram_tensor("dbg_k0", [128, 512], BF16, kind="ExternalOutput").ap()
            p.dma("q", "stdbg", dq1, qkT[8][:, 0:512], reads=["qkT8"])
            dq2 = nc.dram_tensor("dbg_q0", [128, 512], BF16, kind="ExternalOutput").ap()
            p.dma("q", "stdbg", dq2, qkT[0][:, 0:512], reads=["qkT0"])
            dq3 = nc.dram_tensor("dbg_va0", [128, HEADS * 65], BF16, kind="ExternalOutput").ap()
            p.dma("q", "stdbg", dq3, vaug[0][:], reads=["vaug0"])

        def emit_score(h, qc, bs):
            db = h % 2
            ktile, qtile, poff = 8 + h // 2, h // 2, (h % 2) * 64
            sb_ = bs % 3
            p.op("t", lambda t, bs=bs, qc=qc, sb_=sb_, ktile=ktile, qtile=qtile,
                 poff=poff: t.matmul(
                     PS[sb_][:, 0:QCH],
                     qkT[ktile][poff:poff + 64, bs * 128:(bs + 1) * 128],
                     qkT[qtile][poff:poff + 64, qc * QCH:(qc + 1) * QCH],
                     start=True, stop=True),
                 reads=[f"qkT{ktile}", f"qkT{qtile}"], writes=[f"PS{sb_}"])
            c0 = qc * QCH - 128 * bs + ZOFF - 127
            ab = bs % 2
            exq = X_Q if (h == 0 and qc == 0 and bs == 0) else []
            p.op("s", lambda s, sb_=sb_, ab=ab: s.activation(
                exps[ab][:, 0:QCH], PS[sb_][:, 0:QCH], Act.Exp),
                reads=[f"PS{sb_}"], writes=[f"exps{ab}"] + exq)
            at = bs % 3
            p.op("v", lambda v, ab=ab, at=at, db=db, c0=c0: v.tensor_tensor(
                attT[at][:, 0:QCH], exps[ab][:, 0:QCH],
                dstrip[db][:, c0:c0 + QCH], op=Alu.mult),
                reads=[f"exps{ab}", f"dstrip{db}"], writes=[f"attT{at}"])

        def emit_av(h, qc, bs, nbs):
            avb = 3 + (qc % 2)
            at = bs % 3
            p.op("t", lambda t, bs=bs, at=at, avb=avb, h=h, nbs=nbs: t.matmul(
                PS[avb][0:65, 0:QCH], vaug[bs][:, h * 65:h * 65 + 65],
                attT[at][:, 0:QCH], start=(bs == 0), stop=(bs == nbs - 1)),
                reads=[f"vaug{bs}", f"attT{at}"], writes=[f"PS{avb}"])

        def emit_denom(h, qc):
            avb = 3 + (qc % 2)
            poff = (h % 2) * 64
            dp = (h * NQC + qc) % 2
            dnr_ap = (gs_b if dp == 0 else xn)[64:65, 0:QCH]
            dnrk = f"dnr{dp}"
            dnb_ap = xn[0:64, 0:QCH] if dp == 0 else xn[0:64, QCH:2 * QCH]
            dnbk = f"dnb{dp}"
            ps5h = PS[5][0:64, 0:QCH] if dp == 0 else PS[5][64:128, 0:QCH]
            ps5k = f"PS5{'ab'[dp]}"
            ones_ap = ones64[64:65, :]
            avn_ap = (avn if dp == 0 else avn2)[0:64, 0:QCH]
            avnk = "avn" if dp == 0 else "avn2"
            first = dn_first.pop(dp, False)
            exd = (["gs_b", "xn", "PS5"] if first else [])
            exa = ((["vwbuf6", "vwbuf7"] if dp else []) + X_Q if first else [])
            p.op("s", lambda s, avb=avb, dnr_ap=dnr_ap: s.copy(
                dnr_ap, PS[avb][64:65, 0:QCH]),
                reads=[f"PS{avb}"], writes=[dnrk] + exd)
            p.op("v", lambda v, dnr_ap=dnr_ap: v.reciprocal(dnr_ap, dnr_ap),
                 reads=[dnrk], writes=[dnrk])
            p.op("t", lambda t, ps5h=ps5h, dnr_ap=dnr_ap, ones_ap=ones_ap: t.matmul(
                ps5h, ones_ap, dnr_ap, start=True, stop=True),
                 reads=[dnrk, "ones64"], writes=[ps5k] + (["PS5"] if first else []))
            p.op("s", lambda s, dnb_ap=dnb_ap, ps5h=ps5h: s.copy(dnb_ap, ps5h),
                 reads=[ps5k], writes=[dnbk])
            p.op("v", lambda v, avb=avb, avn_ap=avn_ap, dnb_ap=dnb_ap: v.tensor_tensor(
                avn_ap, PS[avb][0:64, 0:QCH], dnb_ap,
                op=Alu.mult), reads=[f"PS{avb}", dnbk], writes=[avnk] + exa)
            p.dma("q", "stav",
                  xaT[h // 2][poff:poff + 64, qc * QCH:(qc + 1) * QCH],
                  avn_ap, reads=[avnk], writes=[f"xaT{h // 2}"])

        pend_denom = None
        dn_first = {0: True, 1: True}
        for h in range(HEADS):
            db = h % 2
            p.dma("q", f"ldd{db}", dstrip[db][:], dbias_d.ap()[h, :, 127:ZLEN],
                  reads=["dbias_dram"], writes=[f"dstrip{db}"])
            for qc in range(NQC):
                nbs = (qc * QCH + QCH - 1) // 128 + 1
                lag = 2 if OPT_AVLAG else 0
                for bs in range(nbs):
                    emit_score(h, qc, bs)
                    if bs >= lag:
                        emit_av(h, qc, bs - lag, nbs)
                    if bs == 2 and pend_denom is not None:
                        emit_denom(*pend_denom)
                        pend_denom = None
                for bs in range(max(0, nbs - lag), nbs):
                    emit_av(h, qc, bs, nbs)
                if pend_denom is not None:
                    emit_denom(*pend_denom)
                    pend_denom = None
                if OPT_DENOMLAG:
                    pend_denom = (h, qc)
                else:
                    emit_denom(h, qc)
            if b == 0 and OPT_PREPIL:
                for _ in range(2 if h % 8 < 4 else 3):
                    nt_ = next(ffnw_iter, None)
                    if nt_ is not None:
                        prep_tile(*nt_)
        if pend_denom is not None:
            emit_denom(*pend_denom)
            pend_denom = None
        if b == 0:
            while True:
                nt_ = next(ffnw_iter, None)
                if nt_ is None:
                    break
                prep_tile(*nt_)

        if dbg and b == 0:
            dq4 = nc.dram_tensor("dbg_ao0", [128, 512], BF16, kind="ExternalOutput").ap()
            p.dma("q", "stdbg", dq4, xaT[0][:, 0:512], reads=["xaT0"])
            dq5 = nc.dram_tensor("dbg_aoall", [8, 128, 64], BF16, kind="ExternalOutput").ap()
            for _i in range(8):
                p.dma("q", "stdbg", dq5[_i], xaT[_i][:, 0:64], reads=[f"xaT{_i}"])

        for ct in range(8):
            pjb = ct % 2
            extraw = X_A if ct == 0 else []
            p.dma("q", f"ldpj{pjb}", projbuf[pjb][:].rearrange("p (c n) -> p c n", c=8),
                  projT_d[:, :, ct * 128:(ct + 1) * 128].rearrange("c p n -> p c n"),
                  reads=["projT_dram"], writes=[f"projbuf{pjb}"] + extraw)
            for tc4 in range(NTC):
                pb = (ct + tc4) % 2
                for Ct in range(8):
                    p.op("t", lambda t, ct=ct, tc4=tc4, Ct=Ct, pb=pb, pjb=pjb: t.matmul(
                        PS[pb][:, 0:TC], projbuf[pjb][:, Ct * 128:(Ct + 1) * 128],
                        xaT[Ct][:, tc4 * TC:(tc4 + 1) * TC],
                        start=(Ct == 0), stop=(Ct == 7)),
                        reads=[f"projbuf{pjb}", f"xaT{Ct}"], writes=[f"PS{pb}"])
                sb_ = (ct + tc4) % 2
                p.op("v", lambda v, ct=ct, pb=pb, sb_=sb_: v.tensor_scalar(
                    pstage[sb_][:, 0:TC], PS[pb][:, 0:TC], projb_c[:, ct:ct + 1], None,
                    op0=Alu.add), reads=[f"PS{pb}", "projb_c"], writes=[f"pstage{sb_}"])
                p.dma("q", "stpp", ppart_d[b, ct, :, tc4 * TC:(tc4 + 1) * TC],
                      pstage[sb_][:, 0:TC], reads=[f"pstage{sb_}"], writes=["pp_dram"])

        for ct in range(8):
            extraw = (A_ATTN if ct == 0 else [])
            p.dma("q", "ldfw", wgq[ct][:], wgqT_d[ct, :, :], reads=["wgq_dram"],
                  writes=[f"wgq{ct}"] + extraw)
            p.dma("q", "ldfw", wuq[ct][:], wuqT_d[ct, :, :], reads=["wuq_dram"],
                  writes=[f"wuq{ct}"])
        def emit_hq_tr(tq):
            for it in range(NI):
                pb = [4, 5][it % 2]
                p.op("t", lambda t, it=it, pb=pb: t.matmul(
                    PS[pb][:, 0:128], wstage2[:, it * 128:(it + 1) * 128], ident_b[:],
                    start=True, stop=True),
                    reads=["wstage2", "ident_b"],
                    writes=[f"PS{pb}"] + (["PS5a", "PS5b"] if pb == 5 else []))
                p.op("s", lambda s, it=it, tq=tq, pb=pb: s.copy(
                    hqT[it][:, tq * 128:(tq + 1) * 128], PS[pb][:, 0:128]),
                    reads=[f"PS{pb}"], writes=[f"hqT{it}"])

        for tc4 in range(NTC):
            pend_tr = []
            for tq in range(NTQ):
                tt = tc4 * NTQ + tq
                xo = (tt % 2) * 8
                for ct in range(8):
                    extraw = B_ATTN if (tc4 == 0 and tq == 0 and ct == 0) else []
                    p.dma("q", "ldxq", xnq_s[xo + ct][:],
                          xnqT_d[b, ct, :, tt * 128:(tt + 1) * 128],
                          reads=["xnq_dram"], writes=[f"xnq_s{xo + ct}"] + extraw)
                for ic in range(NIC):
                    iw = min(512, FFI - ic * 512)
                    pg = [0, 2][ic % 2]
                    pu = [1, 3][ic % 2]
                    for ct in range(8):
                        p.op("t", lambda t, ct=ct, ic=ic, iw=iw, xo=xo, pg=pg: t.matmul(
                            PS[pg][:, 0:iw], xnq_s[xo + ct][:],
                            wgq[ct][:, ic * 512:ic * 512 + iw],
                            start=(ct == 0), stop=(ct == 7)),
                            reads=[f"xnq_s{xo + ct}", f"wgq{ct}"], writes=[f"PS{pg}"])
                    for ct in range(8):
                        p.op("t", lambda t, ct=ct, ic=ic, iw=iw, xo=xo, pu=pu: t.matmul(
                            PS[pu][:, 0:iw], xnq_s[xo + ct][:],
                            wuq[ct][:, ic * 512:ic * 512 + iw],
                            start=(ct == 0), stop=(ct == 7)),
                            reads=[f"xnq_s{xo + ct}", f"wuq{ct}"], writes=[f"PS{pu}"])
                    p.op("s", lambda s, iw=iw, pg=pg: s.activation(gs_b[:, 0:iw],
                         PS[pg][:, 0:iw], Act.Silu), reads=[f"PS{pg}"],
                         writes=["gs_b"] + (["dnr0"] if (tc4 == 0 and tq == 0
                                                         and ic == 0) else []))
                    p.op("v", lambda v, ic=ic, iw=iw, pu=pu: v.tensor_tensor(
                        fbuf0[:, ic * 512:ic * 512 + iw], gs_b[:, 0:iw],
                        PS[pu][:, 0:iw], op=Alu.mult),
                        reads=["gs_b", f"PS{pu}"],
                        writes=["fbuf0"] + (["xtile0", "xtile1"]
                                            if (tc4 == 0 and tq == 0 and ic == 0) else []))
                while pend_tr:
                    emit_hq_tr(pend_tr.pop(0))
                quant("fbuf0", fbuf0[:, 0:FFI], FFI, wstage2[:, 0:FFI], "wstage2")
                pend_tr.append(tq)
            while pend_tr:
                emit_hq_tr(pend_tr.pop(0))
            for ct in range(8):
                wb_ = ct % 2
                exd = ["qdqw"] if (ct == 0 and tc4 == 0) else []
                p.dma("q", f"ldwd{wb_}", wdbuf[wb_][:].rearrange("p (c n) -> p c n", c=NI),
                      wdqT_d[:, :, ct * 128:(ct + 1) * 128].rearrange("c p n -> p c n"),
                      reads=["wdq_dram"], writes=[f"wdbuf{wb_}"] + exd)
                pb2 = 2 + (ct % 2)
                for it in range(NI):
                    p.op("t", lambda t, ct=ct, it=it, pb2=pb2, wb_=wb_: t.matmul(
                        PS[pb2][:, 0:TC], wdbuf[wb_][:, it * 128:(it + 1) * 128],
                        hqT[it][:, 0:TC], start=(it == 0), stop=(it == NI - 1)),
                        reads=[f"wdbuf{wb_}", f"hqT{it}"], writes=[f"PS{pb2}"])
                sb_ = ct % 2
                p.dma("q", f"ldpp{sb_}", ppbuf[sb_][:, 0:TC],
                      ppart_d[b, ct, :, tc4 * TC:(tc4 + 1) * TC],
                      reads=["pp_dram"], writes=[f"ppbuf{sb_}"])
                p.dma("q", f"ldxf{sb_}", xfbuf[sb_][:, 0:TC],
                      x_fm[b, ct * 128:(ct + 1) * 128, tc4 * TC:(tc4 + 1) * TC],
                      writes=[f"xfbuf{sb_}"])
                exo = (X_A + X_Q + X_P) if (tc4 == 0 and ct < 2) else []
                p.op("v", lambda v, sb_=sb_, pb2=pb2: v.tensor_tensor(
                    ostage[sb_][:, 0:TC], PS[pb2][:, 0:TC], ppbuf[sb_][:, 0:TC], op=Alu.add),
                    reads=[f"PS{pb2}", f"ppbuf{sb_}"], writes=[f"ostage{sb_}"] + exo)
                p.op("v", lambda v, sb_=sb_: v.tensor_tensor(
                    ostage[sb_][:, 0:TC], ostage[sb_][:, 0:TC], xfbuf[sb_][:, 0:TC], op=Alu.add),
                    reads=[f"xfbuf{sb_}"], writes=[f"ostage{sb_}"])
                p.dma("q", "stout",
                      out_fm[b, ct * 128:(ct + 1) * 128, tc4 * TC:(tc4 + 1) * TC],
                      ostage[sb_][:, 0:TC], reads=[f"ostage{sb_}"], writes=["out_dram"])

    if dbg:
        dbgout = {
            "dbg_wq": ([128, FQKV], BF16, wqkvT_d[0, :, :], "wqkvT_dram"),
            "dbg_db": ([128, ZLEN - 127], BF16, dbias_d.ap()[0, :, 127:ZLEN], "dbias_dram"),
            "dbg_xnq": ([128, T], BF16, xnqT_d[0, 0, :, :], "xnq_dram"),
            "dbg_pp": ([128, T], F32, ppart_d[0, 0, :, :], "pp_dram"),
            "dbg_xn": ([128, C], F32, xn[:, 0:C], "xn"),
            "dbg_ws2": ([128, FFI], BF16, wstage2[:, 0:FFI], "wstage2"),
            "dbg_bna": ([128, 2], F32, bna[:], "bna"),
            "dbg_aff": ([128, 4], F32, aff[:], "aff0"),
            "dbg_vt": ([128, FFI], F32, vt_[:], "vt"),
            "dbg_qnb": ([128, FFI], BF16, qnb[:], "qnb"),
            "dbg_sblk": ([128, FFI // 16], F32, sblk[:], "sblk"),
            "dbg_iblk": ([128, FFI // 16], F32, iblk[:], "iblk"),
            "dbg_mblk": ([128, FFI // 16], F32, mblk[:], "mblk"),
        }
        for nm, (shp, dt_, srcap, key) in dbgout.items():
            od = nc.dram_tensor(nm, shp, dt_, kind="ExternalOutput").ap()
            p.dma("q", "stdbg", od, srcap, reads=[key], writes=[])
        p.barrier("t", "stdbg")

    p.barrier("t", "stout")

    @blk.sync
    def _(e):
        p.emit("q", e)

    @blk.vector
    def _(e):
        p.emit("v", e)

    @blk.scalar
    def _(e):
        p.emit("s", e)

    @blk.tensor
    def _(e):
        p.emit("t", e)

    @blk.gpsimd
    def _(e):
        p.emit("g", e)

    p.ctx.close()
    if ret_p:
        return nc, p
    return nc


_CACHE = {}


def _get_nc(T, NB):
    key = (T, NB)
    if key not in _CACHE:
        _CACHE[key] = build_program(T, NB)
    return _CACHE[key]


def kernel(L0_attn_x, L0_attn_qkv_w, L0_attn_qkv_b, L0_attn_proj_w, L0_attn_proj_b,
           L0_attn_rel_pos_emb, L0_ffn_w_gate, L0_ffn_w_up, L0_ffn_w_down):
    x = np.ascontiguousarray(np.asarray(L0_attn_x, dtype=np.float32))
    B, T, _ = x.shape
    NCORES = 8
    NB = B // NCORES
    nc = _get_nc(T, NB)
    w = {
        "qkv_w": np.ascontiguousarray(np.asarray(L0_attn_qkv_w, dtype=np.float32)),
        "qkv_b": np.ascontiguousarray(np.asarray(L0_attn_qkv_b, dtype=np.float32)).reshape(1, -1),
        "proj_w": np.ascontiguousarray(np.asarray(L0_attn_proj_w, dtype=np.float32)),
        "proj_b": np.ascontiguousarray(np.asarray(L0_attn_proj_b, dtype=np.float32)).reshape(1, -1),
        "rel": np.ascontiguousarray(np.asarray(L0_attn_rel_pos_emb, dtype=np.float32)),
        "wg": np.ascontiguousarray(np.asarray(L0_ffn_w_gate, dtype=np.float32)),
        "wu": np.ascontiguousarray(np.asarray(L0_ffn_w_up, dtype=np.float32)),
        "wd": np.ascontiguousarray(np.asarray(L0_ffn_w_down, dtype=np.float32)),
    }
    in_maps = []
    for c in range(NCORES):
        xs = np.ascontiguousarray(x[c * NB:(c + 1) * NB])
        m = dict(w)
        m["x_tm"] = xs
        m["x_fm"] = np.ascontiguousarray(xs.transpose(0, 2, 1))
        in_maps.append(m)
    res = run_bass_kernel_spmd(nc, in_maps, core_ids=list(range(NCORES)))
    outs = [r["out_fm"].transpose(0, 2, 1) for r in res.results]
    return np.ascontiguousarray(np.concatenate(outs, axis=0)).astype(np.float32)


# version 13 (final)
# speedup vs baseline: 1.0439x; 1.0421x over previous
import sys
sys.path.insert(0, '/opt/trn_rl_repo')
import contextlib
import numpy as np
import concourse.bass as bass
import concourse.mybir as mybir
from concourse.bass_utils import run_bass_kernel_spmd

F32 = mybir.dt.float32
BF16 = mybir.dt.bfloat16
Alu = mybir.AluOpType
Act = mybir.ActivationFunctionType
AX = mybir.AxisListType.X

C = 1024
HEADS = 16
HD = 64
FQKV = 3 * C
FFI = 2816
MAXLEN = 1024
NI = FFI // 128
NTHR = [0.25, 0.75, 1.25, 1.75, 2.5, 3.5, 5.0]
NW = [0.5, 0.5, 0.5, 0.5, 1.0, 1.0, 2.0]
GPSIMD_QUANT = True
OPT_AVLAG = True
OPT_DENOMLAG = True
OPT_PREPIL = True


class Prog:

    def __init__(self, nc):
        self.nc = nc
        self.ctx = contextlib.ExitStack()
        self.eng = {"v": nc.vector, "s": nc.scalar, "t": nc.tensor,
                    "g": nc.gpsimd, "q": nc.sync}
        self.semobj = {}
        self.cnt = {}
        self.is_dma = {}
        for e in "vstg":
            self.semobj[e] = self.ctx.enter_context(nc.semaphore("sem_" + e))
            self.cnt[e] = 0
            self.is_dma[e] = False
        self.know = {e: {} for e in "vstgq"}
        self.lastw = {}
        self.readers = {}
        self.snap = {}
        self.queue = {e: [] for e in "vstgq"}
        self.log = []

    def dma_sem(self, name):
        if name not in self.semobj:
            self.semobj[name] = self.ctx.enter_context(self.nc.semaphore(name))
            self.cnt[name] = 0
            self.is_dma[name] = True
        return name

    def _waits(self, e, reads, writes, fw=False):
        need = {}
        for b in reads:
            t = self.lastw.get(b)
            if t is not None:
                need[t[0]] = max(need.get(t[0], 0), t[1])
        for b in writes:
            t = self.lastw.get(b)
            if t is not None:
                need[t[0]] = max(need.get(t[0], 0), t[1])
            for rt in self.readers.get(b, ()):
                need[rt[0]] = max(need.get(rt[0], 0), rt[1])
        kn = self.know[e]
        for sname in sorted(need):
            val = need[sname]
            if sname == e and not fw and not getattr(self, 'FW_ALL', False):
                continue
            if self.is_dma[sname]:
                val = self.cnt[sname]
            if kn.get(sname, 0) >= val:
                continue
            mul = 16 if self.is_dma[sname] else 1
            self.queue[e].append(("w", sname, val * mul))
            sn = self.snap.get((sname, val))
            if sn:
                for k2, v2 in sn.items():
                    if kn.get(k2, 0) < v2:
                        kn[k2] = v2
            kn[sname] = max(kn.get(sname, 0), val)

    def _finish(self, e, semname, reads, writes, inc):
        self.cnt[semname] += 1
        tag = (semname, self.cnt[semname])
        sn = dict(self.know[e])
        sn[semname] = self.cnt[semname]
        if e in "vstg":
            sn[e] = self.cnt.get(e, 0)
        self.snap[tag] = sn
        for b in writes:
            self.lastw[b] = tag
            self.readers[b] = []
        for b in reads:
            self.readers.setdefault(b, []).append(tag)

    def op(self, e, f, reads=(), writes=(), fw=False):
        assert e in "vstg"
        n0 = len(self.queue[e])
        self._waits(e, reads, writes, fw=fw)
        self.log.append((e, "op", tuple(reads), tuple(writes),
                         tuple(self.queue[e][n0:])))
        sem = self.semobj[e]
        self.queue[e].append(("i", f, sem, 1))
        self._finish(e, e, reads, writes, 1)

    def dma(self, e, semname, out, in_, reads=(), writes=()):
        self.dma_sem(semname)
        n0 = len(self.queue[e])
        self._waits(e, reads, writes)
        self.log.append((e, "dma:" + semname, tuple(reads), tuple(writes),
                         tuple(self.queue[e][n0:])))
        sem = self.semobj[semname]
        self.queue[e].append(("d", out, in_, sem))
        self._finish(e, semname, reads, writes, 16)

    def barrier(self, e, semname):
        val = self.cnt.get(semname, 0)
        if val == 0:
            return
        kn = self.know[e]
        if kn.get(semname, 0) >= val:
            return
        mul = 16 if self.is_dma[semname] else 1
        self.queue[e].append(("w", semname, val * mul))
        kn[semname] = val

    def emit(self, e, engine):
        for item in self.queue[e]:
            if item[0] == "w":
                engine.wait_ge(self.semobj[item[1]], item[2])
            elif item[0] == "i":
                inst = item[1](engine)
                inst.then_inc(item[2], item[3])
            else:
                inst = engine.dma_start(out=item[1], in_=item[2])
                inst.then_inc(item[3], 16)


def build_program(T=2048, NB=2, dbg=False, ret_p=False):
    QCH = min(512, T)
    TC = QCH
    NT = T // 128
    NTC = T // TC
    NTQ = TC // 128
    NQC = T // QCH
    ZOFF = QCH - 1
    ZLEN = T + ZOFF
    DW = 127 + ZLEN
    NIC = (FFI + 511) // 512
    TW2 = T * 2 // 4
    assert T % 512 == 0 or T == QCH

    nc = bass.Bass("TRN2", target_bir_lowering=False, detect_race_conditions=False)
    p = Prog(nc)

    x_tm = nc.dram_tensor("x_tm", [NB, T, C], F32, kind="ExternalInput").ap()
    x_fm = nc.dram_tensor("x_fm", [NB, C, T], F32, kind="ExternalInput").ap()
    qkv_w = nc.dram_tensor("qkv_w", [FQKV, C], F32, kind="ExternalInput").ap()
    qkv_b = nc.dram_tensor("qkv_b", [1, FQKV], F32, kind="ExternalInput").ap()
    proj_w = nc.dram_tensor("proj_w", [C, C], F32, kind="ExternalInput").ap()
    proj_b = nc.dram_tensor("proj_b", [1, C], F32, kind="ExternalInput").ap()
    rel = nc.dram_tensor("rel", [2 * MAXLEN - 1, C], F32, kind="ExternalInput").ap()
    wg_d = nc.dram_tensor("wg", [FFI, C], F32, kind="ExternalInput").ap()
    wu_d = nc.dram_tensor("wu", [FFI, C], F32, kind="ExternalInput").ap()
    wd_d = nc.dram_tensor("wd", [C, FFI], F32, kind="ExternalInput").ap()
    out_fm = nc.dram_tensor("out_fm", [NB, C, T], F32, kind="ExternalOutput").ap()

    wqkvT_d = nc.dram_tensor("wqkvT_d", [8, 128, FQKV], BF16, kind="Internal").ap()
    projT_d = nc.dram_tensor("projT_d", [8, 128, C], BF16, kind="Internal").ap()
    wgqT_d = nc.dram_tensor("wgqT_d", [8, 128, FFI], BF16, kind="Internal").ap()
    wuqT_d = nc.dram_tensor("wuqT_d", [8, 128, FFI], BF16, kind="Internal").ap()
    wdqT_d = nc.dram_tensor("wdqT_d", [NI, 128, C], BF16, kind="Internal").ap()
    tcf_d = nc.dram_tensor("tcf_d", [HEADS, ZLEN], F32, kind="Internal")
    dbias_d = nc.dram_tensor("dbias_d", [HEADS, 128, DW], BF16, kind="Internal")
    xnqT_d = nc.dram_tensor("xnqT_d", [NB, 8, 128, T], BF16, kind="Internal").ap()
    ppart_d = nc.dram_tensor("ppart_d", [NB, 8, 128, T], F32, kind="Internal").ap()

    MTOTAL = [0]

    def mk(name, shape, dtype, off, seg=0):
        nbytes = shape[1] * (2 if dtype == BF16 else 4)
        start = MBASE + seg + off
        MTOTAL[0] = max(MTOTAL[0], seg + off + nbytes)
        return nc.alloc_sbuf_tensor_at(name, list(shape), dtype, offset=start)

    qkT_b0 = 4096 if T >= 2048 else T * 2
    _A_sz = max(16 * qkT_b0 + 16 * 2112, 16 * 5632, 10240 + 8192)
    _SW = ZLEN - 127
    _Xoff = 8 * qkT_b0 + 2 * _SW * 2
    _B_sz = max(_Xoff + 8192, 22 * 1024 + 4096 + 11264 + 3 * 4096)
    _TOT = _A_sz + _B_sz + 4096 + 28672 + 24576
    MBASE = (nc.sbuf_top - _TOT) // 32 * 32

    SA = 0
    qkT_b = qkT_b0
    A_attn = 16 * qkT_b + 16 * 2112
    A_ffn = 16 * 5632
    A_sz = max(A_attn, A_ffn, 10240 + 8192)
    qkT = [mk(f"qkT{i}", [128, T], BF16, i * qkT_b, SA) for i in range(16)]
    vaug = [mk(f"vaug{i}", [128, HEADS * 65], BF16, 16 * qkT_b + i * 2112, SA)
            for i in range(NT)]
    wgq = [mk(f"wgq{i}", [128, FFI], BF16, i * 5632, SA) for i in range(8)]
    wuq = [mk(f"wuq{i}", [128, FFI], BF16, 45056 if False else 8 * 5632 + i * 5632, SA)
           for i in range(8)] if A_sz >= 16 * 5632 else None
    if wuq is None:
        wuq = [mk(f"wuq{i}", [128, FFI], BF16, 8 * 5632 + i * 5632, SA) for i in range(8)]
    tcf = mk("tcf", [128, ZLEN], F32, 0, SA)
    tblT = mk("tblT", [128, 2048], F32, ZLEN * 4 + 32 - (ZLEN * 4) % 32, SA)
    A_P0 = ["tblT", "tcf"]
    A_ATTN = [f"qkT{i}" for i in range(16)] + [f"vaug{i}" for i in range(NT)]
    A_FFN = [f"wgq{i}" for i in range(8)] + [f"wuq{i}" for i in range(8)]

    SB = SA + A_sz
    SW = ZLEN - 127
    xaT_b = 4096 if T >= 2048 else T * 2
    ds_b = SW * 2
    Xoff = 8 * xaT_b + 2 * ds_b
    B_attn = Xoff + 8192
    B_ffn = 22 * 1024 + 4096 + 11264 + 3 * 4096
    B_sz = max(B_attn, B_ffn)
    xaT = [mk(f"xaT{i}", [128, T], BF16, i * xaT_b, SB) for i in range(8)]
    dstrip = [mk(f"dstrip{i}", [128, SW], BF16, 8 * xaT_b + i * ds_b, SB) for i in range(2)]
    wqbuf = [mk(f"wqbuf{i}", [128, 1024], BF16, Xoff + i * 2048, SB) for i in range(2)]
    vwbuf = [mk(f"vwbuf{i}", [128, 512], BF16, Xoff + i * 1024, SB) for i in range(8)]
    attT = [mk(f"attT{i}", [128, QCH], BF16, Xoff + i * 1024, SB) for i in range(3)]
    exps = [mk(f"exps{i}", [128, QCH], BF16, Xoff + 4928 + i * 1024, SB) for i in range(2)]
    avn = mk("avn", [128, QCH], BF16, Xoff + 3 * 1024, SB)
    projbuf = [mk(f"projbuf{i}", [128, 1024], BF16, Xoff + i * 2048, SB) for i in range(2)]
    ident_f = mk("ident_f", [128, 128], F32, Xoff + 4096, SB)
    onesb = mk("onesb", [128, 128], BF16, Xoff + 4608, SB)
    tstage = mk("tstage", [128, 16], F32, Xoff + 4864, SB)
    hqT = [mk(f"hqT{i}", [128, 512], BF16, i * 1024, SB) for i in range(NI)]
    xnq_s = [mk(f"xnq_s{i}", [128, 128], BF16, 22 * 1024 + i * 256, SB) for i in range(16)]
    wdbuf = [mk(f"wdbuf{i}", [128, FFI], BF16, 26624 + i * 5632, SB) for i in range(2)]
    ppbuf = [mk(f"ppbuf{i}", [128, 512], F32, 37888 + i * 2048, SB) for i in range(2)]
    xfbuf = [mk(f"xfbuf{i}", [128, 512], F32, 41984 + i * 2048, SB) for i in range(2)]
    ostage = [mk(f"ostage{i}", [128, 512], F32, 46080 + i * 2048, SB) for i in range(2)]
    qdqw = mk("qdqw", [128, FFI], BF16, 26624, SB)
    X_Q = ["wqbuf0", "wqbuf1"] + [f"vwbuf{i}" for i in range(8)]
    X_A = ["attT0", "attT1", "attT2", "avn", "attf0", "attf1", "exps0", "exps1"]
    X_P = ["pstage0", "pstage1", "projbuf0", "projbuf1"]
    B_ATTN = ([f"xaT{i}" for i in range(8)] + ["dstrip0", "dstrip1"] + X_Q + X_A + X_P
              + ["qdqw", "ident_f", "onesb", "tstage"])
    B_FFN = ([f"hqT{i}" for i in range(NI)] + ["ppbuf0", "ppbuf1", "xfbuf0", "xfbuf1",
              "ostage0", "ostage1", "wdbuf0", "wdbuf1"] + [f"xnq_s{i}" for i in range(16)])

    SC = SB + B_sz
    attf = [mk(f"attf{i}", [128, QCH], F32, i * 2048, SC) for i in range(2)]
    pstage = [mk(f"pstage{i}", [128, 512], F32, i * 2048, SC) for i in range(2)]
    B32_sz = 4096

    SD = SC + B32_sz
    vt_ = mk("vt", [128, FFI], F32, 0, SD)
    fbuf0 = mk("fbuf0", [128, FFI], F32, 11264, SD)
    fcast = mk("fcast", [128, FFI], BF16, 11264, SD)
    fcast2 = mk("fcast2", [128, FFI], BF16, 16896, SD)
    xtile = [mk(f"xtile{i}", [128, C], F32, 11264 + i * 4096, SD) for i in range(2)]
    gs_b = mk("gs_b", [128, 512], F32, 22528, SD)
    xn = mk("xn", [128, C], F32, 24576, SD)
    C32_sz = 28672
    dnr = gs_b
    dnb = xn

    SE = SD + C32_sz
    qnb = mk("qnb", [128, FFI], BF16, 0, SE)
    cmb = mk("cmb", [128, FFI], BF16, 5632, SE)
    sgn_b = mk("sgn", [128, FFI], BF16, 11264, SE)
    wstage2 = mk("wstage2", [128, FFI], BF16, 16896, SE)
    wstage = mk("wstage", [128, 1024], BF16, 16896, SE)
    wstage3 = mk("wstage3", [128, 1024], BF16, 18944, SE)
    tcb16 = mk("tcb16", [128, ZLEN], BF16, 16896, SE)
    xnb = mk("xnb", [128, C], BF16, 22528, SE)
    qdq = mk("qdq", [128, C], BF16, 22528, SE)
    C16_sz = 24576

    mblk = p.ctx.enter_context(nc.sbuf_tensor("mblk", [128, FFI // 16], F32))
    sblk = p.ctx.enter_context(nc.sbuf_tensor("sblk", [128, FFI // 16], F32))
    iblk = p.ctx.enter_context(nc.sbuf_tensor("iblk", [128, FFI // 16], F32))
    bns = p.ctx.enter_context(nc.sbuf_tensor("bns", [128, 12], F32))
    bna = p.ctx.enter_context(nc.sbuf_tensor("bna", [128, 2], F32))
    aff = p.ctx.enter_context(nc.sbuf_tensor("aff", [128, 4], F32))
    qkvb_c = p.ctx.enter_context(nc.sbuf_tensor("qkvb_c", [128, 24], F32))
    projb_c = p.ctx.enter_context(nc.sbuf_tensor("projb_c", [128, 8], F32))
    vbb = p.ctx.enter_context(nc.sbuf_tensor("vbb", [128, C], BF16))
    ident_b = p.ctx.enter_context(nc.sbuf_tensor("ident_b", [128, 128], BF16))
    ones64 = p.ctx.enter_context(nc.sbuf_tensor("ones64", [128, 64], F32))

    PS = [p.ctx.enter_context(nc.psum_tensor(f"ps{i}", [128, 512], F32)) for i in range(6)]
    PS6B = p.ctx.enter_context(nc.psum_tensor("ps6b", [128, 1024], BF16))
    PS7B = p.ctx.enter_context(nc.psum_tensor("ps7b", [128, 1024], BF16))
    PS = PS + [PS6B, PS7B]
    PSB = [None] * 6 + [PS6B[:, 0:128], PS7B[:, 0:128]]

    blk = p.ctx.enter_context(nc.Block())

    p.op("g", lambda g: g.memset(onesb[:], 1.0), writes=["onesb"])
    p.op("g", lambda g: g.affine_select(ident_b[:], onesb[:], pattern=[[-1, 128]],
         compare_op=Alu.is_equal, fill=0.0, base=0, channel_multiplier=1),
         reads=["onesb"], writes=["ident_b"])
    p.op("g", lambda g: g.affine_select(ident_f[:], onesb[:], pattern=[[-1, 128]],
         compare_op=Alu.is_equal, fill=0.0, base=0, channel_multiplier=1),
         reads=["onesb"], writes=["ident_f"])
    p.op("v", lambda v: v.memset(ones64[:], 1.0), writes=["ones64"])

    qb_src = bass.AP(tensor=qkv_b.tensor, offset=0, ap=[[128, 24], [1, 128]])
    p.dma("q", "ld0", gs_b[0:24, 0:128], qb_src, writes=["gs_b"])
    p.op("t", lambda t: t.transpose(PS[5][:, 0:24], gs_b[0:24, 0:128], ident_f[0:24, 0:24]),
         reads=["gs_b", "ident_f"], writes=["PS5"])
    p.op("s", lambda s: s.copy(qkvb_c[:], PS[5][:, 0:24]), reads=["PS5"], writes=["qkvb_c"])
    pb_src = bass.AP(tensor=proj_b.tensor, offset=0, ap=[[128, 8], [1, 128]])
    p.dma("q", "ld0", gs_b[0:8, 128:256], pb_src, writes=["gs_b"])
    p.op("t", lambda t: t.transpose(PS[5][:, 0:8], gs_b[0:8, 128:256], ident_f[0:8, 0:8]),
         reads=["gs_b", "ident_f"], writes=["PS5"])
    p.op("s", lambda s: s.copy(projb_c[:], PS[5][:, 0:8]), reads=["PS5"], writes=["projb_c"])
    p.dma("g", "ldvb", vbb[:], qkv_b[0, 2 * C:3 * C].unsqueeze(0).broadcast_to([128, C]),
          writes=["vbb"])

    for wt in range(24 + 8):
        is_proj = wt >= 24
        src = proj_w if is_proj else qkv_w
        row = (wt - 24 if is_proj else wt) * 128
        fcb, fck = (fcast, "fcast") if wt % 2 == 0 else (fcast2, "fcast2")
        wsb, wsk = (wstage, "wstage") if wt % 2 == 0 else (wstage3, "wstage3")
        p.dma("g", "ldwg", fcb[:, 0:C], src[row:row + 128, :], writes=[fck])
        for ct in range(8):
            pb = ct % 6
            p.op("t", lambda t, ct=ct, pb=pb, fcb=fcb: t.matmul(
                PS[pb][:, 0:128], fcb[:, ct * 128:(ct + 1) * 128], ident_b[:],
                start=True, stop=True),
                reads=[fck, "ident_b"], writes=[f"PS{pb}"])
            p.op("s", lambda s, ct=ct, pb=pb, wsb=wsb: s.copy(
                wsb[:, ct * 128:(ct + 1) * 128], PS[pb][:, 0:128]),
                reads=[f"PS{pb}"], writes=[wsk])
        dstd = projT_d if is_proj else wqkvT_d
        dstk = "projT_dram" if is_proj else "wqkvT_dram"
        p.dma("q", "stw", dstd[:, :, row:row + 128].rearrange("c p n -> p c n"),
              wsb[:].rearrange("p (c n) -> p c n", c=8),
              reads=[wsk], writes=[dstk])

    nrt = (2 * MAXLEN - 1 + 127) // 128
    for rt in range(nrt):
        row0 = min(rt * 128, 2 * MAXLEN - 1 - 128)
        p.dma("q", "ldr", fbuf0[:, 0:C], rel[row0:row0 + 128, :],
              writes=["fbuf0"] + (["fcast", "fcast2"] if row0 == 0 else []))
        p.op("v", lambda v: v.tensor_reduce(
            tstage[:, 0:16], fbuf0[:, 0:C].rearrange("p (h d) -> p h d", d=HD),
            axis=AX, op=Alu.add), reads=["fbuf0"], writes=["tstage"])
        p.op("t", lambda t: t.transpose(PS[5][0:16, 0:128], tstage[:, 0:16], ident_f[:]),
             reads=["tstage", "ident_f"], writes=["PS5"])
        p.op("s", lambda s, row0=row0: s.copy(tblT[0:16, row0:row0 + 128],
                                              PS[5][0:16, 0:128]),
             reads=["PS5"], writes=["tblT"])
    p.op("v", lambda v: v.memset(tcf[0:16, 0:ZOFF], -30000.0), writes=["tcf"])
    biaslen = min(1022, T - 1) + 1
    p.op("v", lambda v: v.tensor_copy(tcf[0:16, ZOFF:ZOFF + biaslen],
                                      tblT[0:16, 1023:1023 + biaslen]),
         reads=["tblT"], writes=["tcf"])
    if ZOFF + biaslen < ZLEN:
        fl = ZLEN - ZOFF - biaslen
        p.op("v", lambda v, fl=fl: v.tensor_scalar(
            tcf[0:16, ZOFF + biaslen:ZLEN], tcf[0:16, 0:fl], 0.0,
            tblT[0:16, 2046:2047], op0=Alu.mult, op1=Alu.add),
            reads=["tblT", "tcf"], writes=["tcf"])
    p.op("s", lambda s: s.activation(tcf[0:16, 0:ZLEN], tcf[0:16, 0:ZLEN], Act.Exp),
         reads=["tcf"], writes=["tcf"], fw=True)
    p.dma("q", "sttc", tcf_d.ap()[:, :], tcf[0:16, 0:ZLEN], reads=["tcf"],
          writes=["tcf_dram"])
    for h in range(HEADS):
        p.dma("g", "ldtc", tcb16[:],
              tcf_d.ap()[h, :].unsqueeze(0).broadcast_to([128, ZLEN]),
              reads=["tcf_dram"],
              writes=["wstage2"] + (["wstage", "wstage3"] if h == 0 else []))
        ddst = bass.AP(tensor=dbias_d, offset=h * 128 * DW, ap=[[DW + 1, 128], [1, ZLEN]])
        p.dma("q", "std", ddst, tcb16[:], reads=["wstage2"], writes=["dbias_dram"])

    def quant(src_key, src_ap, width, dst_ap, dst_key, extra_reads=()):
        er = list(extra_reads)
        nch = 2 if width > 1024 else 1
        cw = width // nch
        first = True
        for c in range(nch):
            nb = cw // 16
            b0 = c * (cw // 16)
            x3 = src_ap[:, c * cw:(c + 1) * cw].rearrange("p (b k) -> p b k", k=16)
            sl = (slice(None), slice(c * cw, (c + 1) * cw))
            mb = mblk[:, b0:b0 + nb]
            sb_ = sblk[:, b0:b0 + nb]
            ibl = iblk[:, b0:b0 + nb]
            mk_, sk_, ik_ = f"mblk{c}", f"sblk{c}", f"iblk{c}"
            p.op("v", lambda v, x3=x3, mb=mb, nb=nb: v.tensor_reduce(
                mb, x3, axis=AX, op=Alu.max, apply_absolute_value=True),
                reads=[src_key] + (er if first else []), writes=[mk_])
            p.op("v", lambda v, mb=mb, sb_=sb_: v.tensor_scalar(
                sb_, mb, 1.0 / 6.0, 1e-12, op0=Alu.mult, op1=Alu.max),
                reads=[mk_], writes=[sk_])
            p.op("v", lambda v, sb_=sb_, ibl=ibl: v.reciprocal(ibl, sb_),
                 reads=[sk_], writes=[ik_])
            ib3 = ibl.unsqueeze(2).broadcast_to([128, nb, 16])
            sb3 = sb_.unsqueeze(2).broadcast_to([128, nb, 16])
            vtc = vt_[sl]
            p.op("g", lambda g, x3=x3, ib3=ib3, vtc=vtc: g.tensor_tensor(
                vtc.rearrange("p (b k) -> p b k", k=16), x3, ib3, op=Alu.mult),
                reads=[src_key, ik_] + (er if first else []), writes=[f"vt{c}"])
            p.op("v", lambda v, vtc=vtc, c=c, cw=cw: v.tensor_scalar(
                cmb[:, c * cw:(c + 1) * cw], vtc, 0.0, 2.0,
                op0=Alu.is_gt, op1=Alu.mult), reads=[f"vt{c}"], writes=[f"cmb{c}"])
            p.op("v", lambda v, c=c, cw=cw: v.tensor_scalar(
                sgn_b[:, c * cw:(c + 1) * cw], cmb[:, c * cw:(c + 1) * cw], -1.0,
                None, op0=Alu.add), reads=[f"cmb{c}"], writes=[f"sgn{c}"])
            p.op("s", lambda s, vtc=vtc: s.square(vtc, vtc),
                 reads=[f"vt{c}"], writes=[f"vt{c}"], fw=False)
            p.op("v", lambda v, vtc=vtc, c=c, cw=cw: v.tensor_scalar(
                qnb[:, c * cw:(c + 1) * cw], vtc, NTHR[0] ** 2, NW[0],
                op0=Alu.is_gt, op1=Alu.mult), reads=[f"vt{c}"], writes=[f"qnb{c}"])
            for i in range(1, 7):
                p.op("v", lambda v, i=i, vtc=vtc, c=c, cw=cw: v.tensor_scalar(
                    cmb[:, c * cw:(c + 1) * cw], vtc, NTHR[i] ** 2, NW[i],
                    op0=Alu.is_gt, op1=Alu.mult), reads=[f"vt{c}"], writes=[f"cmb{c}"])
                p.op("v", lambda v, c=c, cw=cw: v.tensor_tensor(
                    qnb[:, c * cw:(c + 1) * cw], qnb[:, c * cw:(c + 1) * cw],
                    cmb[:, c * cw:(c + 1) * cw], op=Alu.add),
                    reads=[f"cmb{c}"], writes=[f"qnb{c}"])
            p.op("v", lambda v, c=c, cw=cw: v.tensor_tensor(
                qnb[:, c * cw:(c + 1) * cw], qnb[:, c * cw:(c + 1) * cw],
                sgn_b[:, c * cw:(c + 1) * cw], op=Alu.mult),
                reads=[f"sgn{c}"], writes=[f"qnb{c}"])
            p.op("g", lambda g, c=c, cw=cw, sb3=sb3, nb=nb: g.tensor_tensor(
                dst_ap[:, c * cw:(c + 1) * cw].rearrange("p (b k) -> p b k", k=16),
                qnb[:, c * cw:(c + 1) * cw].rearrange("p (b k) -> p b k", k=16),
                sb3, op=Alu.mult),
                reads=[f"qnb{c}", sk_],
                writes=[dst_key] + (["xnb"] if dst_key == "qdq" and first else []))
            first = False

    def prep_tile(src_dram, wt, width, nct, dst_dram, dkey, p0=False):
        st = wstage if nct == 8 else wstage2
        stk = "wstage" if nct == 8 else "wstage2"
        dst_q = qdqw if width == FFI else qdq
        dqk = "qdqw" if width == FFI else "qdq"
        p.dma("q", "ldf", fbuf0[:, 0:width], src_dram[wt * 128:(wt + 1) * 128, :],
              writes=["fbuf0"])
        quant("fbuf0", fbuf0[:, 0:width], width, dst_q[:, 0:width], dqk)
        for ct in range(nct):
            if p0:
                pb = ct % 6
                p.op("t", lambda t, ct=ct, pb=pb, dst_q=dst_q: t.matmul(
                    PS[pb][:, 0:128], dst_q[:, ct * 128:(ct + 1) * 128], ident_b[:],
                    start=True, stop=True),
                    reads=[dqk, "ident_b"], writes=[f"PS{pb}"])
                p.op("s", lambda s, ct=ct, pb=pb, st=st: s.copy(
                    st[:, ct * 128:(ct + 1) * 128], PS[pb][:, 0:128]),
                    reads=[f"PS{pb}"], writes=[stk])
                continue
            pb = 6 + (ct % 2)
            p.op("t", lambda t, ct=ct, pb=pb, dst_q=dst_q: t.transpose(
                PSB[pb], dst_q[:, ct * 128:(ct + 1) * 128], ident_b[:]),
                reads=[dqk, "ident_b"], writes=[f"PS{pb}"])
            p.op("s", lambda s, ct=ct, pb=pb, st=st: s.copy(
                st[:, ct * 128:(ct + 1) * 128], PSB[pb]),
                reads=[f"PS{pb}"], writes=[stk])
        p.dma("q", "stw", dst_dram[:, :, wt * 128:(wt + 1) * 128].rearrange("c p n -> p c n"),
              st[:, 0:nct * 128].rearrange("p (c n) -> p c n", c=nct),
              reads=[stk], writes=[dkey])

    for wt in range(8):
        prep_tile(wd_d, wt, FFI, NI, wdqT_d, "wdq_dram", p0=True)
    ffnw_tiles = ([(wg_d, wt, C, 8, wgqT_d, "wgq_dram") for wt in range(NI)]
                  + [(wu_d, wt, C, 8, wuqT_d, "wuq_dram") for wt in range(NI)])
    ffnw_iter = iter(ffnw_tiles)

    for b in range(NB):
        for tt in range(NT):
            fb = tt % 2
            if tt == 0:
                extraw = (A_FFN + B_FFN) if b > 0 else ["qdqw"]
            else:
                extraw = []
            p.dma("q", f"ldx{fb}", xtile[fb][:], x_tm[b, tt * 128:(tt + 1) * 128, :],
                  writes=[f"xtile{fb}"] + (["fbuf0"] if tt < 2 else []))
            p.op("v", lambda v, fb=fb: v.bn_stats(bns[:, 0:6], xtile[fb][:, 0:512]),
                 reads=[f"xtile{fb}"], writes=["bns"])
            p.op("v", lambda v, fb=fb: v.bn_stats(bns[:, 6:12], xtile[fb][:, 512:1024]),
                 reads=[f"xtile{fb}"], writes=["bns"])
            p.op("v", lambda v: v.bn_aggr(bna[:], bns[:]), reads=["bns"], writes=["bna"],
                 fw=True)
            p.op("v", lambda v: v.tensor_scalar(aff[:, 2:3], bna[:, 1:2], 1e-5, None,
                 op0=Alu.add), reads=["bna"], writes=["aff2"], fw=True)
            p.op("s", lambda s: s.sqrt(aff[:, 3:4], aff[:, 2:3]), reads=["aff2"],
                 writes=["aff3"])
            p.op("v", lambda v: v.reciprocal(aff[:, 0:1], aff[:, 3:4]),
                 reads=["aff3"], writes=["aff0"], fw=True)
            p.op("v", lambda v: v.tensor_tensor(aff[:, 1:2], bna[:, 0:1], aff[:, 0:1],
                 op=Alu.mult), reads=["bna", "aff0"], writes=["aff1"], fw=True)
            p.op("v", lambda v: v.tensor_scalar(aff[:, 1:2], aff[:, 1:2], -1.0, None,
                 op0=Alu.mult), reads=["aff1"], writes=["aff1"], fw=True)
            p.op("v", lambda v, fb=fb: v.tensor_scalar(xn[:, 0:C], xtile[fb][:],
                 aff[:, 0:1], aff[:, 1:2], op0=Alu.mult, op1=Alu.add),
                 reads=[f"xtile{fb}", "aff0", "aff1"], writes=["xn"], fw=True)
            p.op("v", lambda v: v.tensor_copy(xnb[:, 0:C], xn[:, 0:C]),
                 reads=["xn"], writes=["xnb"])
            for ct in range(8):
                pb = ct % 6
                p.op("t", lambda t, ct=ct, pb=pb: t.matmul(
                    PS[pb][:, 0:128], xnb[:, ct * 128:(ct + 1) * 128], ident_b[:],
                    start=True, stop=True),
                    reads=["xnb", "ident_b"], writes=[f"PS{pb}"])
                p.op("s", lambda s, ct=ct, tt=tt, pb=pb: s.copy(
                    xaT[ct][:, tt * 128:(tt + 1) * 128], PS[pb][:, 0:128]),
                    reads=[f"PS{pb}"], writes=[f"xaT{ct}"] + (extraw if ct == 0 else []))
            quant("xn", xn[:, 0:C], C, qdq[:, 0:C], "qdq")
            for ct in range(8):
                pb = (ct + 2) % 6
                p.op("t", lambda t, ct=ct, pb=pb: t.matmul(
                    PS[pb][:, 0:128], qdq[:, ct * 128:(ct + 1) * 128], ident_b[:],
                    start=True, stop=True),
                    reads=["qdq", "ident_b"], writes=[f"PS{pb}"])
                p.op("s", lambda s, ct=ct, pb=pb: s.copy(
                    wstage[:, ct * 128:(ct + 1) * 128], PS[pb][:, 0:128]),
                    reads=[f"PS{pb}"], writes=["wstage"])
            p.dma("q", "stxq", xnqT_d[b, :, :, tt * 128:(tt + 1) * 128].rearrange("c p n -> p c n"),
                  wstage[:].rearrange("p (c n) -> p c n", c=8),
                  reads=["wstage"], writes=["xnq_dram"])

        for ft in range(16):
            fb = ft % 2
            extraw = (X_A + X_P + [f"vwbuf{i}" for i in range(8)]) if (ft == 0 and b > 0) else []
            p.dma("q", f"ldq{fb}",
                  wqbuf[fb][:].rearrange("p (c n) -> p c n", c=8),
                  wqkvT_d[:, :, ft * 128:(ft + 1) * 128].rearrange("c p n -> p c n"),
                  reads=["wqkvT_dram"],
                  writes=[f"wqbuf{fb}"] + extraw)
            for tc4 in range(NTC):
                pb = (ft + tc4) % 2
                for ct in range(8):
                    p.op("t", lambda t, ft=ft, tc4=tc4, ct=ct, fb=fb, pb=pb: t.matmul(
                        PS[pb][:, 0:TC], wqbuf[fb][:, ct * 128:(ct + 1) * 128],
                        xaT[ct][:, tc4 * TC:(tc4 + 1) * TC],
                        start=(ct == 0), stop=(ct == 7)),
                        reads=[f"wqbuf{fb}", f"xaT{ct}"], writes=[f"PS{pb}"])
                is_q = ft < 8
                exw = [f"qkT{ft}"]
                if tc4 == 0 and ft == 0:
                    exw = exw + (A_FFN if b > 0 else A_P0)
                p.op("v", lambda v, ft=ft, tc4=tc4, pb=pb, is_q=is_q: v.tensor_scalar(
                    qkT[ft][:, tc4 * TC:(tc4 + 1) * TC], PS[pb][:, 0:TC],
                    qkvb_c[:, ft:ft + 1], (0.125 if is_q else 1.0),
                    op0=Alu.add, op1=Alu.mult),
                    reads=[f"PS{pb}", "qkvb_c"], writes=exw)
        for tt in range(NT):
            p.op("v", lambda v, tt=tt: v.memset(vaug[tt][:], 1.0), writes=[f"vaug{tt}"])
        for fc in range(2):
            for ct in range(8):
                exv = (["wqbuf0", "wqbuf1", "ident_f", "onesb", "tstage"]
                       if (fc == 0 and ct == 0) else [])
                p.dma("q", "ldvw", vwbuf[ct][:],
                      wqkvT_d[ct, :, 2 * C + fc * 512:2 * C + (fc + 1) * 512],
                      reads=["wqkvT_dram"], writes=[f"vwbuf{ct}"] + exv)
            for tt in range(NT):
                pb = tt % 2
                for ct in range(8):
                    p.op("t", lambda t, tt=tt, ct=ct, pb=pb: t.matmul(
                        PS[pb][:, 0:512], xaT[ct][:, tt * 128:(tt + 1) * 128],
                        vwbuf[ct][:], start=(ct == 0), stop=(ct == 7)),
                        reads=[f"vwbuf{ct}", f"xaT{ct}"], writes=[f"PS{pb}"])
                p.op("v", lambda v, tt=tt, fc=fc, pb=pb: v.tensor_tensor(
                    vaug[tt][:].rearrange("p (h e) -> p h e", e=65)[:, fc * 8:(fc + 1) * 8, 0:64],
                    PS[pb][:, 0:512].rearrange("p (h d) -> p h d", d=64),
                    vbb[:, fc * 512:(fc + 1) * 512].rearrange("p (h d) -> p h d", d=64),
                    op=Alu.add), reads=[f"PS{pb}", "vbb"], writes=[f"vaug{tt}"])

        if dbg and b == 0:
            dq1 = nc.dram_tensor("dbg_k0", [128, 512], BF16, kind="ExternalOutput").ap()
            p.dma("q", "stdbg", dq1, qkT[8][:, 0:512], reads=["qkT8"])
            dq2 = nc.dram_tensor("dbg_q0", [128, 512], BF16, kind="ExternalOutput").ap()
            p.dma("q", "stdbg", dq2, qkT[0][:, 0:512], reads=["qkT0"])
            dq3 = nc.dram_tensor("dbg_va0", [128, HEADS * 65], BF16, kind="ExternalOutput").ap()
            p.dma("q", "stdbg", dq3, vaug[0][:], reads=["vaug0"])

        def emit_score(h, qc, bs):
            db = h % 2
            ktile, qtile, poff = 8 + h // 2, h // 2, (h % 2) * 64
            sb_ = bs % 3
            p.op("t", lambda t, bs=bs, qc=qc, sb_=sb_, ktile=ktile, qtile=qtile,
                 poff=poff: t.matmul(
                     PS[sb_][:, 0:QCH],
                     qkT[ktile][poff:poff + 64, bs * 128:(bs + 1) * 128],
                     qkT[qtile][poff:poff + 64, qc * QCH:(qc + 1) * QCH],
                     start=True, stop=True),
                 reads=[f"qkT{ktile}", f"qkT{qtile}"], writes=[f"PS{sb_}"])
            c0 = qc * QCH - 128 * bs + ZOFF - 127
            ab = bs % 2
            exq = X_Q if (h == 0 and qc == 0 and bs == 0) else []
            p.op("s", lambda s, sb_=sb_, ab=ab: s.activation(
                exps[ab][:, 0:QCH], PS[sb_][:, 0:QCH], Act.Exp),
                reads=[f"PS{sb_}"], writes=[f"exps{ab}"] + exq)
            at = bs % 3
            p.op("v", lambda v, ab=ab, at=at, db=db, c0=c0: v.tensor_tensor(
                attT[at][:, 0:QCH], exps[ab][:, 0:QCH],
                dstrip[db][:, c0:c0 + QCH], op=Alu.mult),
                reads=[f"exps{ab}", f"dstrip{db}"], writes=[f"attT{at}"])

        def emit_av(h, qc, bs, nbs):
            avb = 3 + (qc % 2)
            at = bs % 3
            p.op("t", lambda t, bs=bs, at=at, avb=avb, h=h, nbs=nbs: t.matmul(
                PS[avb][0:65, 0:QCH], vaug[bs][:, h * 65:h * 65 + 65],
                attT[at][:, 0:QCH], start=(bs == 0), stop=(bs == nbs - 1)),
                reads=[f"vaug{bs}", f"attT{at}"], writes=[f"PS{avb}"])

        def emit_denom(h, qc):
            avb = 3 + (qc % 2)
            poff = (h % 2) * 64
            p.op("s", lambda s, avb=avb: s.copy(dnr[64:65, 0:QCH],
                                                PS[avb][64:65, 0:QCH]),
                 reads=[f"PS{avb}"], writes=["gs_b"])
            p.op("v", lambda v: v.reciprocal(dnr[64:65, 0:QCH], dnr[64:65, 0:QCH]),
                 reads=["gs_b"], writes=["gs_b"])
            p.op("t", lambda t: t.matmul(PS[5][0:64, 0:QCH], ones64[64:65, :],
                                         dnr[64:65, 0:QCH], start=True, stop=True),
                 reads=["gs_b", "ones64"], writes=["PS5"])
            p.op("s", lambda s: s.copy(dnb[0:64, 0:QCH], PS[5][0:64, 0:QCH]),
                 reads=["PS5"], writes=["xn"])
            p.op("v", lambda v, avb=avb: v.tensor_tensor(
                avn[0:64, 0:QCH], PS[avb][0:64, 0:QCH], dnb[0:64, 0:QCH],
                op=Alu.mult), reads=[f"PS{avb}", "xn"], writes=["avn"])
            p.dma("q", "stav",
                  xaT[h // 2][poff:poff + 64, qc * QCH:(qc + 1) * QCH],
                  avn[0:64, 0:QCH], reads=["avn"], writes=[f"xaT{h // 2}"])

        pend_denom = None
        for h in range(HEADS):
            db = h % 2
            p.dma("q", f"ldd{db}", dstrip[db][:], dbias_d.ap()[h, :, 127:ZLEN],
                  reads=["dbias_dram"], writes=[f"dstrip{db}"])
            for qc in range(NQC):
                nbs = (qc * QCH + QCH - 1) // 128 + 1
                lag = 2 if OPT_AVLAG else 0
                for bs in range(nbs):
                    emit_score(h, qc, bs)
                    if bs >= lag:
                        emit_av(h, qc, bs - lag, nbs)
                    if bs == 2 and pend_denom is not None:
                        emit_denom(*pend_denom)
                        pend_denom = None
                for bs in range(max(0, nbs - lag), nbs):
                    emit_av(h, qc, bs, nbs)
                if pend_denom is not None:
                    emit_denom(*pend_denom)
                    pend_denom = None
                if OPT_DENOMLAG:
                    pend_denom = (h, qc)
                else:
                    emit_denom(h, qc)
            if b == 0 and OPT_PREPIL:
                for _ in range(2 if h % 8 < 4 else 3):
                    nt_ = next(ffnw_iter, None)
                    if nt_ is not None:
                        prep_tile(*nt_)
        if pend_denom is not None:
            emit_denom(*pend_denom)
            pend_denom = None
        if b == 0:
            while True:
                nt_ = next(ffnw_iter, None)
                if nt_ is None:
                    break
                prep_tile(*nt_)

        if dbg and b == 0:
            dq4 = nc.dram_tensor("dbg_ao0", [128, 512], BF16, kind="ExternalOutput").ap()
            p.dma("q", "stdbg", dq4, xaT[0][:, 0:512], reads=["xaT0"])
            dq5 = nc.dram_tensor("dbg_aoall", [8, 128, 64], BF16, kind="ExternalOutput").ap()
            for _i in range(8):
                p.dma("q", "stdbg", dq5[_i], xaT[_i][:, 0:64], reads=[f"xaT{_i}"])

        for ct in range(8):
            pjb = ct % 2
            extraw = X_A if ct == 0 else []
            p.dma("q", f"ldpj{pjb}", projbuf[pjb][:].rearrange("p (c n) -> p c n", c=8),
                  projT_d[:, :, ct * 128:(ct + 1) * 128].rearrange("c p n -> p c n"),
                  reads=["projT_dram"], writes=[f"projbuf{pjb}"] + extraw)
            for tc4 in range(NTC):
                pb = (ct + tc4) % 2
                for Ct in range(8):
                    p.op("t", lambda t, ct=ct, tc4=tc4, Ct=Ct, pb=pb, pjb=pjb: t.matmul(
                        PS[pb][:, 0:TC], projbuf[pjb][:, Ct * 128:(Ct + 1) * 128],
                        xaT[Ct][:, tc4 * TC:(tc4 + 1) * TC],
                        start=(Ct == 0), stop=(Ct == 7)),
                        reads=[f"projbuf{pjb}", f"xaT{Ct}"], writes=[f"PS{pb}"])
                sb_ = (ct + tc4) % 2
                p.op("v", lambda v, ct=ct, pb=pb, sb_=sb_: v.tensor_scalar(
                    pstage[sb_][:, 0:TC], PS[pb][:, 0:TC], projb_c[:, ct:ct + 1], None,
                    op0=Alu.add), reads=[f"PS{pb}", "projb_c"], writes=[f"pstage{sb_}"])
                p.dma("q", "stpp", ppart_d[b, ct, :, tc4 * TC:(tc4 + 1) * TC],
                      pstage[sb_][:, 0:TC], reads=[f"pstage{sb_}"], writes=["pp_dram"])

        for ct in range(8):
            extraw = (A_ATTN if ct == 0 else [])
            p.dma("q", "ldfw", wgq[ct][:], wgqT_d[ct, :, :], reads=["wgq_dram"],
                  writes=[f"wgq{ct}"] + extraw)
            p.dma("q", "ldfw", wuq[ct][:], wuqT_d[ct, :, :], reads=["wuq_dram"],
                  writes=[f"wuq{ct}"])
        def emit_hq_tr(tq):
            for it in range(NI):
                pb = [4, 5, 2, 3][it % 4]
                p.op("t", lambda t, it=it, pb=pb: t.matmul(
                    PS[pb][:, 0:128], wstage2[:, it * 128:(it + 1) * 128], ident_b[:],
                    start=True, stop=True),
                    reads=["wstage2", "ident_b"], writes=[f"PS{pb}"])
                p.op("s", lambda s, it=it, tq=tq, pb=pb: s.copy(
                    hqT[it][:, tq * 128:(tq + 1) * 128], PS[pb][:, 0:128]),
                    reads=[f"PS{pb}"], writes=[f"hqT{it}"])

        for tc4 in range(NTC):
            pend_tr = []
            for tq in range(NTQ):
                tt = tc4 * NTQ + tq
                xo = (tt % 2) * 8
                for ct in range(8):
                    extraw = B_ATTN if (tc4 == 0 and tq == 0 and ct == 0) else []
                    p.dma("q", "ldxq", xnq_s[xo + ct][:],
                          xnqT_d[b, ct, :, tt * 128:(tt + 1) * 128],
                          reads=["xnq_dram"], writes=[f"xnq_s{xo + ct}"] + extraw)
                for ic in range(NIC):
                    iw = min(512, FFI - ic * 512)
                    for ct in range(8):
                        p.op("t", lambda t, ct=ct, ic=ic, iw=iw, xo=xo: t.matmul(
                            PS[0][:, 0:iw], xnq_s[xo + ct][:],
                            wgq[ct][:, ic * 512:ic * 512 + iw],
                            start=(ct == 0), stop=(ct == 7)),
                            reads=[f"xnq_s{xo + ct}", f"wgq{ct}"], writes=["PS0"])
                    for ct in range(8):
                        p.op("t", lambda t, ct=ct, ic=ic, iw=iw, xo=xo: t.matmul(
                            PS[1][:, 0:iw], xnq_s[xo + ct][:],
                            wuq[ct][:, ic * 512:ic * 512 + iw],
                            start=(ct == 0), stop=(ct == 7)),
                            reads=[f"xnq_s{xo + ct}", f"wuq{ct}"], writes=["PS1"])
                    p.op("s", lambda s, iw=iw: s.activation(gs_b[:, 0:iw],
                         PS[0][:, 0:iw], Act.Silu), reads=["PS0"], writes=["gs_b"])
                    p.op("v", lambda v, ic=ic, iw=iw: v.tensor_tensor(
                        fbuf0[:, ic * 512:ic * 512 + iw], gs_b[:, 0:iw],
                        PS[1][:, 0:iw], op=Alu.mult),
                        reads=["gs_b", "PS1"],
                        writes=["fbuf0"] + (["xtile0", "xtile1"]
                                            if (tc4 == 0 and tq == 0 and ic == 0) else []))
                while pend_tr:
                    emit_hq_tr(pend_tr.pop(0))
                quant("fbuf0", fbuf0[:, 0:FFI], FFI, wstage2[:, 0:FFI], "wstage2")
                pend_tr.append(tq)
            while pend_tr:
                emit_hq_tr(pend_tr.pop(0))
            for ct in range(8):
                wb_ = ct % 2
                exd = ["qdqw"] if (ct == 0 and tc4 == 0) else []
                p.dma("g", f"ldwd{wb_}", wdbuf[wb_][:].rearrange("p (c n) -> p c n", c=NI),
                      wdqT_d[:, :, ct * 128:(ct + 1) * 128].rearrange("c p n -> p c n"),
                      reads=["wdq_dram"], writes=[f"wdbuf{wb_}"] + exd)
                pb2 = 2 + (ct % 2)
                for it in range(NI):
                    p.op("t", lambda t, ct=ct, it=it, pb2=pb2, wb_=wb_: t.matmul(
                        PS[pb2][:, 0:TC], wdbuf[wb_][:, it * 128:(it + 1) * 128],
                        hqT[it][:, 0:TC], start=(it == 0), stop=(it == NI - 1)),
                        reads=[f"wdbuf{wb_}", f"hqT{it}"], writes=[f"PS{pb2}"])
                sb_ = ct % 2
                p.dma("q", f"ldpp{sb_}", ppbuf[sb_][:, 0:TC],
                      ppart_d[b, ct, :, tc4 * TC:(tc4 + 1) * TC],
                      reads=["pp_dram"], writes=[f"ppbuf{sb_}"])
                p.dma("q", f"ldxf{sb_}", xfbuf[sb_][:, 0:TC],
                      x_fm[b, ct * 128:(ct + 1) * 128, tc4 * TC:(tc4 + 1) * TC],
                      writes=[f"xfbuf{sb_}"])
                exo = (X_A + X_Q + X_P) if (tc4 == 0 and ct < 2) else []
                p.op("v", lambda v, sb_=sb_, pb2=pb2: v.tensor_tensor(
                    ostage[sb_][:, 0:TC], PS[pb2][:, 0:TC], ppbuf[sb_][:, 0:TC], op=Alu.add),
                    reads=[f"PS{pb2}", f"ppbuf{sb_}"], writes=[f"ostage{sb_}"] + exo)
                p.op("v", lambda v, sb_=sb_: v.tensor_tensor(
                    ostage[sb_][:, 0:TC], ostage[sb_][:, 0:TC], xfbuf[sb_][:, 0:TC], op=Alu.add),
                    reads=[f"xfbuf{sb_}"], writes=[f"ostage{sb_}"])
                p.dma("q", "stout",
                      out_fm[b, ct * 128:(ct + 1) * 128, tc4 * TC:(tc4 + 1) * TC],
                      ostage[sb_][:, 0:TC], reads=[f"ostage{sb_}"], writes=["out_dram"])

    if dbg:
        dbgout = {
            "dbg_wq": ([128, FQKV], BF16, wqkvT_d[0, :, :], "wqkvT_dram"),
            "dbg_db": ([128, ZLEN - 127], BF16, dbias_d.ap()[0, :, 127:ZLEN], "dbias_dram"),
            "dbg_xnq": ([128, T], BF16, xnqT_d[0, 0, :, :], "xnq_dram"),
            "dbg_pp": ([128, T], F32, ppart_d[0, 0, :, :], "pp_dram"),
            "dbg_xn": ([128, C], F32, xn[:, 0:C], "xn"),
            "dbg_ws2": ([128, FFI], BF16, wstage2[:, 0:FFI], "wstage2"),
            "dbg_bna": ([128, 2], F32, bna[:], "bna"),
            "dbg_aff": ([128, 4], F32, aff[:], "aff0"),
            "dbg_vt": ([128, FFI], F32, vt_[:], "vt"),
            "dbg_qnb": ([128, FFI], BF16, qnb[:], "qnb"),
            "dbg_sblk": ([128, FFI // 16], F32, sblk[:], "sblk"),
            "dbg_iblk": ([128, FFI // 16], F32, iblk[:], "iblk"),
            "dbg_mblk": ([128, FFI // 16], F32, mblk[:], "mblk"),
        }
        for nm, (shp, dt_, srcap, key) in dbgout.items():
            od = nc.dram_tensor(nm, shp, dt_, kind="ExternalOutput").ap()
            p.dma("q", "stdbg", od, srcap, reads=[key], writes=[])
        p.barrier("t", "stdbg")

    p.barrier("t", "stout")

    @blk.sync
    def _(e):
        p.emit("q", e)

    @blk.vector
    def _(e):
        p.emit("v", e)

    @blk.scalar
    def _(e):
        p.emit("s", e)

    @blk.tensor
    def _(e):
        p.emit("t", e)

    @blk.gpsimd
    def _(e):
        p.emit("g", e)

    p.ctx.close()
    if ret_p:
        return nc, p
    return nc


_CACHE = {}


def _get_nc(T, NB):
    key = (T, NB)
    if key not in _CACHE:
        _CACHE[key] = build_program(T, NB)
    return _CACHE[key]


def kernel(L0_attn_x, L0_attn_qkv_w, L0_attn_qkv_b, L0_attn_proj_w, L0_attn_proj_b,
           L0_attn_rel_pos_emb, L0_ffn_w_gate, L0_ffn_w_up, L0_ffn_w_down):
    x = np.ascontiguousarray(np.asarray(L0_attn_x, dtype=np.float32))
    B, T, _ = x.shape
    NCORES = 8
    NB = B // NCORES
    nc = _get_nc(T, NB)
    w = {
        "qkv_w": np.ascontiguousarray(np.asarray(L0_attn_qkv_w, dtype=np.float32)),
        "qkv_b": np.ascontiguousarray(np.asarray(L0_attn_qkv_b, dtype=np.float32)).reshape(1, -1),
        "proj_w": np.ascontiguousarray(np.asarray(L0_attn_proj_w, dtype=np.float32)),
        "proj_b": np.ascontiguousarray(np.asarray(L0_attn_proj_b, dtype=np.float32)).reshape(1, -1),
        "rel": np.ascontiguousarray(np.asarray(L0_attn_rel_pos_emb, dtype=np.float32)),
        "wg": np.ascontiguousarray(np.asarray(L0_ffn_w_gate, dtype=np.float32)),
        "wu": np.ascontiguousarray(np.asarray(L0_ffn_w_up, dtype=np.float32)),
        "wd": np.ascontiguousarray(np.asarray(L0_ffn_w_down, dtype=np.float32)),
    }
    in_maps = []
    for c in range(NCORES):
        xs = np.ascontiguousarray(x[c * NB:(c + 1) * NB])
        m = dict(w)
        m["x_tm"] = xs
        m["x_fm"] = np.ascontiguousarray(xs.transpose(0, 2, 1))
        in_maps.append(m)
    res = run_bass_kernel_spmd(nc, in_maps, core_ids=list(range(NCORES)))
    outs = [r["out_fm"].transpose(0, 2, 1) for r in res.results]
    return np.ascontiguousarray(np.concatenate(outs, axis=0)).astype(np.float32)
